# version 02
# baseline (speedup 1.0000x reference)
import math
from contextlib import ExitStack

import numpy as np
import ml_dtypes

import concourse.bass as bass
import concourse.mybir as mybir
from concourse.bass_utils import run_bass_kernel_spmd
from concourse.bass_core_rust import IndirectOffsetOnAxis

F32 = mybir.dt.float32
BF16 = mybir.dt.bfloat16
U32 = mybir.dt.uint32
AF = mybir.ActivationFunctionType
ALU = mybir.AluOpType
AX = mybir.AxisListType

NCORES = 8
B = 2
T = 4096
D = 4096
TPC = 1024
NT = TPC // 128
LN_EPS = 1e-5
RMS_EPS = 1.1920929e-07

bf16 = ml_dtypes.bfloat16



def shard_tokens(x):
    out = []
    for i in range(NCORES):
        out.append(np.ascontiguousarray(
            np.concatenate([x[0, 512 * i:512 * (i + 1)], x[1, 512 * i:512 * (i + 1)]], 0)))
    return out


def unshard_tokens(parts):
    x = np.empty((B, T, D), dtype=parts[0].dtype)
    for i in range(NCORES):
        x[0, 512 * i:512 * (i + 1)] = parts[i][:512]
        x[1, 512 * i:512 * (i + 1)] = parts[i][512:]
    return x



ENGS = ("sync", "vector", "scalar", "tensor", "gpsimd")


class Prog:
    def __init__(self, nc, stack):
        self.nc = nc
        self.stack = stack
        self.q = {e: [] for e in ENGS}
        self.sems = {}
        self.cnt = {}
        self.nblock = 0
        for name in ("dS", "dG", "dA", "eV", "eS", "eT", "eG", "eY",
                     "cc0", "cc1", "cc2", "cc3"):
            self.sems[name] = stack.enter_context(nc.semaphore(name))
            self.cnt[name] = 0
        self.junk = stack.enter_context(nc.sbuf_tensor("junk_bar", [128, 8], F32))
        self.kbuf = stack.enter_context(nc.sbuf_tensor("konst", [128, 16], F32))
        self.kmap = {}
        self.ktok = None

    def konst(self, value):
        v = float(value)
        if v not in self.kmap:
            idx = len(self.kmap)
            assert idx < 16
            self.kmap[v] = idx
            ap = self.kbuf[:, idx:idx + 1]
            self.ktok = self.emit("vector", lambda e, ap=ap, v=v: e.memset(ap, v), inc="eV")
        return self.kbuf[:, self.kmap[v]:self.kmap[v] + 1]

    def tok(self, sem):
        return (sem, self.cnt[sem])

    def emit(self, eng, fn, wait=(), inc=None, inc_by=1):
        waits = [w for w in wait if w is not None and w[1] > 0]
        if inc is not None:
            self.cnt[inc] += inc_by
        sems = self.sems

        def run(e, waits=tuple(waits), fn=fn, inc=inc, inc_by=inc_by):
            for (sname, v) in waits:
                e.wait_ge(sems[sname], v)
            r = fn(e)
            if inc is not None:
                r.then_inc(sems[inc], inc_by)
        self.q[eng].append(run)
        return (inc, self.cnt[inc]) if inc else None

    def dma(self, eng, out, in_, wait=(), sem=None, **kw):
        if sem is None:
            sem = {"sync": "dS", "gpsimd": "dG", "scalar": "dA"}[eng]
        return self.emit(eng, lambda e: e.dma_start(out=out, in_=in_, **kw),
                         wait=wait, inc=sem, inc_by=16)

    def barrier(self):
        j = self.junk
        tv = self.emit("vector", lambda e: e.tensor_copy(j[:, 0:1], j[:, 1:2]), inc="eV")
        ts = self.emit("scalar", lambda e: e.copy(j[:, 2:3], j[:, 3:4]), wait=[tv], inc="eS")
        tg = self.emit("gpsimd", lambda e: e.tensor_copy(j[:, 4:5], j[:, 5:6]), wait=[ts], inc="eG")
        tt = self.emit("tensor", lambda e: e.nop(), wait=[tg], inc="eT")
        ty = self.emit("sync", lambda e: e.nop(), wait=[tt], inc="eY")
        for eng in ("vector", "scalar", "gpsimd", "tensor"):
            self.emit(eng, lambda e: e.nop(), wait=[ty])
        return [ty]

    def flush(self, name=None):
        self.nblock += 1
        with self.nc.Block(name or f"blk{self.nblock}") as block:
            for eng in ENGS:
                fns = self.q[eng]
                if not fns:
                    continue
                getattr(block, eng)(lambda e, fns=tuple(fns): [f(e) for f in fns])
        self.q = {e: [] for e in ENGS}



def emit_layernorm(P, x_ap, out_ap, cols, tmp_bf, aux=None):
    nc = P.nc
    st = P.stack
    c = aux["colbuf"]
    wx = aux["x_ready"]
    t_sum = P.emit("vector", lambda e: e.tensor_reduce(c[:, 0:1], x_ap, axis=AX.X, op=ALU.add),
                   wait=wx, inc="eV")
    t_sq = P.emit("scalar", lambda e: e.activation(out=tmp_bf, in_=x_ap, func=AF.Square,
                                                   accum_out=c[:, 2:3]),
                  wait=wx, inc="eS")
    t_mean = P.emit("scalar", lambda e: e.mul(c[:, 1:2], c[:, 0:1], 1.0 / cols),
                    wait=[t_sum], inc="eS")
    t_m2 = P.emit("scalar", lambda e: e.activation(out=c[:, 3:4], in_=c[:, 1:2], func=AF.Square),
                  wait=[t_mean], inc="eS")
    t_var = P.emit("vector", lambda e: e.scalar_tensor_tensor(
        c[:, 4:5], c[:, 2:3], 1.0 / cols, c[:, 3:4], op0=ALU.mult, op1=ALU.subtract),
        wait=[t_sq, t_m2], inc="eV")
    eps_ap = P.konst(LN_EPS)
    t_std = P.emit("scalar", lambda e: e.activation(out=c[:, 5:6], in_=c[:, 4:5],
                                                    func=AF.Sqrt, bias=eps_ap),
                   wait=[t_var, P.ktok], inc="eS")
    t_rstd = P.emit("vector", lambda e: e.reciprocal(c[:, 6:7], c[:, 5:6]),
                    wait=[t_std], inc="eV")
    t_out = P.emit("vector", lambda e: e.tensor_scalar(
        out_ap, x_ap, c[:, 1:2], c[:, 6:7], op0=ALU.subtract, op1=ALU.mult),
        wait=[t_rstd], inc="eV")
    aux["t_var"] = t_var
    aux["t_out"] = t_out
    return t_out


def emit_transposes(P, src_bf_ap_fn, dst_sb, nchunks, ident, ps_pool, wait_src):
    evac_toks = [P.tok("eV"), P.tok("eV")]
    last = None
    for k in range(nchunks):
        ps = ps_pool[k % 2]
        w = list(wait_src) + [evac_toks[k % 2]]
        t_tr = P.emit("tensor", lambda e, k=k, ps=ps: e.transpose(ps[:, :], src_bf_ap_fn(k), ident[:, :]),
                      wait=w, inc="eT")
        last = P.emit("vector", lambda e, k=k, ps=ps: e.tensor_copy(dst_sb[:, 128 * k:128 * (k + 1)], ps[:, :]),
                      wait=[t_tr], inc="eV")
        evac_toks[k % 2] = last
    return last



def build_stage2(P, io):
    nc = P.nc
    st = P.stack
    x_in, x_out = io["x_in"], io["x_out"]

    sb = lambda name, shape, dt=F32: st.enter_context(nc.sbuf_tensor(name, shape, dt))
    psum = lambda name, shape, dt=F32: st.enter_context(nc.psum_tensor(name, shape, dt))

    ident = sb("s2_ident", [128, 128], BF16)
    w_cat = sb("s2_wcat", [128, 32 * 24], BF16)
    w_branch = sb("s2_wbr", [128, 8 * 1024], BF16)
    base_b = sb("s2_base", [128, 24])
    eye_b = sb("s2_eye", [128, 16])
    gate_b = sb("s2_gate", [128, 2])
    z_bf = [sb(f"s2_zbf{t}", [128, D], BF16) for t in range(NT)]
    zT = sb("s2_zT", [128, D], BF16)
    tmp_bf = sb("s2_tmp", [128, D], BF16)
    colbuf = [sb(f"s2_col{t}", [128, 8]) for t in range(NT)]
    ZA = sb("s2_ZA", [128, NT * 24])
    Zs = sb("s2_Zs", [128, NT * 16])
    u_sb = sb("s2_u", [128, NT * 4])
    v_sb = sb("s2_v", [128, NT * 4])
    m_sb = sb("s2_m", [128, NT * 4])
    e_sb = sb("s2_e", [128, NT * 16])
    s_sb = sb("s2_s", [128, NT * 4])
    ls_sb = sb("s2_ls", [128, NT * 4])
    H = sb("s2_H", [128, NT * 16])
    Hpre = sb("s2_Hpre", [128, NT * 4])
    Hpost = sb("s2_Hpost", [128, NT * 4])
    sm_sc = sb("s2_smsc", [128, 4])
    x_tile = sb("s2_x", [128, D])
    out_t = sb("s2_out", [128, D])
    xp = sb("s2_xp", [128, 1024])
    xp_bf = sb("s2_xpbf", [128, 1024], BF16)
    xpT = sb("s2_xpT", [128, 1024], BF16)
    y_sb = sb("s2_y", [128, 1024])
    ps_tr = [psum("s2_pstr0", [128, 128], BF16), psum("s2_pstr1", [128, 128], BF16)]
    ps_lg = psum("s2_pslg", [128, 24])
    ps_y = psum("s2_psy", [128, 512])

    t_ld_ident = P.dma("sync", ident[:, :], io["ident"][:, :])
    for k in range(32):
        P.dma("gpsimd", w_cat[:, 24 * k:24 * (k + 1)], io["w_cat"][128 * k:128 * (k + 1), :])
    for k in range(8):
        P.dma("gpsimd", w_branch[:, 1024 * k:1024 * (k + 1)], io["w_branch"][128 * k:128 * (k + 1), :])
    P.dma("sync", base_b[:, :], io["bases"][0:1, :].to_broadcast([128, 24]))
    P.dma("sync", eye_b[:, :], io["eye_row"][0:1, :].to_broadcast([128, 16]))
    P.dma("sync", gate_b[:, 0:1], io["gate"][0:1, :].to_broadcast([128, 1]))
    t_consts_s = P.tok("dS")
    t_consts_g = P.tok("dG")

    for t in range(NT):
        t_x = P.dma("sync", x_tile[:, :], x_in[128 * t:128 * (t + 1), :],
                    wait=[P.tok("eV")])
        aux = {"colbuf": colbuf[t], "x_ready": [P.tok("dS")]}
        emit_layernorm(P, x_tile[:, :], z_bf[t][:, :], D, tmp_bf[:, :], aux)
        t_z = aux["t_out"]
        c = colbuf[t]
        t_r1 = P.emit("vector", lambda e, c=c: e.tensor_mul(c[:, 7:8], c[:, 4:5], c[:, 6:7]),
                      wait=[aux["t_out"]], inc="eV")
        t_r2 = P.emit("vector", lambda e, c=c: e.tensor_mul(c[:, 7:8], c[:, 7:8], c[:, 6:7]),
                      wait=[t_r1], inc="eV")
        rms_eps_ap = P.konst(RMS_EPS)
        t_r3 = P.emit("scalar", lambda e, c=c: e.activation(out=c[:, 7:8], in_=c[:, 7:8],
                                                            func=AF.Sqrt, bias=rms_eps_ap),
                      wait=[t_r2, P.ktok], inc="eS")
        t_rms = P.emit("vector", lambda e, c=c: e.reciprocal(c[:, 7:8], c[:, 7:8]),
                       wait=[t_r3], inc="eV")
        t_tr = emit_transposes(P, lambda k, t=t: z_bf[t][:, 128 * k:128 * (k + 1)],
                               zT, 32, ident, ps_tr,
                               wait_src=[t_z, t_ld_ident, P.tok("eT")])
        def mm_logits(e, t=t):
            for k in range(32):
                r = e.matmul(ps_lg[:, :], zT[:, 128 * k:128 * (k + 1)],
                             w_cat[:, 24 * k:24 * (k + 1)], start=(k == 0), stop=(k == 31))
            return r
        t_mm = P.emit("tensor", mm_logits, wait=[t_tr, t_consts_g, P.tok("eV")], inc="eT")
        t_ev = P.emit("vector", lambda e, t=t, c=c: e.tensor_scalar(
            ZA[:, 24 * t:24 * (t + 1)], ps_lg[:, :], c[:, 7:8], None, op0=ALU.mult),
            wait=[t_mm, t_rms], inc="eV")
        P.emit("vector", lambda e, t=t: e.tensor_add(
            ZA[:, 24 * t:24 * (t + 1)], ZA[:, 24 * t:24 * (t + 1)], base_b[:, :]),
            wait=[t_ev, t_consts_s], inc="eV")
    t_za = P.tok("eV")

    za4 = ZA[:, :].rearrange("p (t c) -> p t c", t=NT)
    t_zs = P.emit("vector", lambda e: e.tensor_scalar_mul(
        Zs[:, :].rearrange("p (t c) -> p t c", t=NT), za4[:, :, 0:16], 10.0),
        wait=[t_za], inc="eV")
    P.emit("vector", lambda e: e.memset(u_sb[:, :], 0.0), inc="eV")
    t_v0 = P.emit("vector", lambda e: e.memset(v_sb[:, :], 0.0), inc="eV")

    Z4 = Zs[:, :].rearrange("p (t i j) -> p t i j", t=NT, i=4, j=4)
    Z4T = Z4.transpose([0, 1, 3, 2])
    m4 = m_sb[:, :].rearrange("p (t i) -> p t i", t=NT)
    u4 = u_sb[:, :].rearrange("p (t i) -> p t i", t=NT)
    v4 = v_sb[:, :].rearrange("p (t j) -> p t j", t=NT)
    e4 = e_sb[:, :].rearrange("p (t i j) -> p t i j", t=NT, i=4, j=4)
    e4T = e4.transpose([0, 1, 3, 2])
    shp4 = [128, NT, 4, 4]

    def lse_step(P, Zv, ee, out_flat, bcast_col4, wait):
        vb = bcast_col4.unsqueeze(2).to_broadcast(shp4)
        t1 = P.emit("vector", lambda e: e.tensor_tensor(ee, Zv, vb, ALU.add),
                    wait=wait, inc="eV")
        t2 = P.emit("vector", lambda e: e.tensor_reduce(m4, ee, axis=AX.X, op=ALU.max),
                    wait=[t1], inc="eV")
        mb = m4.unsqueeze(3).to_broadcast(shp4)
        t3 = P.emit("vector", lambda e: e.tensor_tensor(ee, ee, mb, ALU.subtract),
                    wait=[t2], inc="eV")
        t4 = P.emit("scalar", lambda e: e.activation(out=e_sb[:, :], in_=e_sb[:, :], func=AF.Exp),
                    wait=[t3], inc="eS")
        t5 = P.emit("vector", lambda e: e.tensor_reduce(s_sb[:, :].rearrange("p (t i) -> p t i", t=NT),
                                                        ee, axis=AX.X, op=ALU.add),
                    wait=[t4], inc="eV")
        t6 = P.emit("scalar", lambda e: e.activation(out=ls_sb[:, :], in_=s_sb[:, :], func=AF.Ln),
                    wait=[t5], inc="eS")
        t7 = P.emit("vector", lambda e: e.tensor_tensor(ls_sb[:, :], ls_sb[:, :], m_sb[:, :], ALU.add),
                    wait=[t6], inc="eV")
        t8 = P.emit("vector", lambda e: e.tensor_scalar_mul(out_flat, ls_sb[:, :], -1.0),
                    wait=[t7], inc="eV")
        return t8

    t_prev = [t_zs, t_v0]
    for it in range(50):
        tu = lse_step(P, Z4, e4, u_sb[:, :], v4, wait=t_prev)
        tv = lse_step(P, Z4T, e4T, v_sb[:, :], u4, wait=[tu])
        t_prev = [tv]

    ub = u4.unsqueeze(3).to_broadcast(shp4)
    vb2 = v4.unsqueeze(2).to_broadcast(shp4)
    H4 = H[:, :].rearrange("p (t i j) -> p t i j", t=NT, i=4, j=4)
    t_h1 = P.emit("vector", lambda e: e.tensor_tensor(H4, Z4, ub, ALU.add), wait=t_prev, inc="eV")
    t_h2 = P.emit("vector", lambda e: e.tensor_tensor(H4, H4, vb2, ALU.add), wait=[t_h1], inc="eV")
    t_h3 = P.emit("scalar", lambda e: e.activation(out=H[:, :], in_=H[:, :], func=AF.Exp),
                  wait=[t_h2], inc="eS")
    t_g = P.emit("scalar", lambda e: e.activation(out=gate_b[:, 1:2], in_=gate_b[:, 0:1],
                                                  func=AF.Sigmoid),
                 wait=[t_consts_s], inc="eS")
    t_g1 = P.emit("vector", lambda e: e.tensor_scalar(sm_sc[:, 0:1], gate_b[:, 1:2],
                                                      -1.0, 1.0, op0=ALU.mult, op1=ALU.add),
                  wait=[t_g], inc="eV")
    t_h4 = P.emit("vector", lambda e: e.tensor_scalar(H[:, :], H[:, :], gate_b[:, 1:2], None,
                                                      op0=ALU.mult),
                  wait=[t_h3, t_g1], inc="eV")
    t_ei2 = P.emit("vector", lambda e: e.tensor_scalar(
        e_sb[:, 0:16], eye_b[:, :], sm_sc[:, 0:1], None, op0=ALU.mult),
        wait=[t_g1, t_prev[0]], inc="eV")
    Hv = H[:, :].rearrange("p (t c) -> p t c", t=NT)
    eyeb = e_sb[:, 0:16].unsqueeze(1).to_broadcast([128, NT, 16])
    t_H = P.emit("vector", lambda e: e.tensor_tensor(Hv, Hv, eyeb, ALU.add),
                 wait=[t_h4, t_ei2], inc="eV")

    def softmax4(P, src_view, dst, wait):
        d4 = dst[:, :].rearrange("p (t i) -> p t i", t=NT)
        mm = m_sb[:, :].rearrange("p (t i) -> p t i", t=NT)[:, :, 0:1]
        ss = s_sb[:, :].rearrange("p (t i) -> p t i", t=NT)[:, :, 0:1]
        rr = ls_sb[:, :].rearrange("p (t i) -> p t i", t=NT)[:, :, 0:1]
        t1 = P.emit("vector", lambda e: e.tensor_reduce(mm, src_view, axis=AX.X, op=ALU.max),
                    wait=wait, inc="eV")
        t2 = P.emit("vector", lambda e: e.tensor_tensor(
            d4, src_view, mm.to_broadcast([128, NT, 4]), ALU.subtract),
            wait=[t1], inc="eV")
        t3 = P.emit("scalar", lambda e: e.activation(out=dst[:, :], in_=dst[:, :], func=AF.Exp),
                    wait=[t2], inc="eS")
        t4 = P.emit("vector", lambda e: e.tensor_reduce(ss, d4, axis=AX.X, op=ALU.add),
                    wait=[t3], inc="eV")
        t5 = P.emit("vector", lambda e: e.reciprocal(rr, ss), wait=[t4], inc="eV")
        t6 = P.emit("vector", lambda e: e.tensor_tensor(
            d4, d4, rr.to_broadcast([128, NT, 4]), ALU.mult),
            wait=[t5], inc="eV")
        return t6

    t_hpre = softmax4(P, za4[:, :, 16:20], Hpre, [t_za, t_H])
    t_hpost = softmax4(P, za4[:, :, 20:24], Hpost, [t_hpre])

    Hpre4 = Hpre[:, :]
    Hpost4 = Hpost[:, :]
    for t in range(NT):
        zt = z_bf[t]
        t_xp = P.emit("vector", lambda e, t=t, zt=zt: e.tensor_scalar(
            xp[:, :], zt[:, 0:1024], Hpre4[:, 4 * t:4 * t + 1], None, op0=ALU.mult),
            wait=[t_hpre, P.tok("eS")], inc="eV")
        for n in range(1, 4):
            t_xp = P.emit("vector", lambda e, t=t, n=n, zt=zt: e.scalar_tensor_tensor(
                xp[:, :], zt[:, 1024 * n:1024 * (n + 1)], Hpre4[:, 4 * t + n:4 * t + n + 1],
                xp[:, :], op0=ALU.mult, op1=ALU.add),
                wait=[t_xp], inc="eV")
        t_xpbf = P.emit("scalar", lambda e: e.copy(xp_bf[:, :], xp[:, :]), wait=[t_xp], inc="eS")
        t_xpT = emit_transposes(P, lambda k: xp_bf[:, 128 * k:128 * (k + 1)],
                                xpT, 8, ident, ps_tr, wait_src=[t_xpbf, P.tok("eT")])
        t_ymm = []
        for cc in range(2):
            def mm_y(e, cc=cc):
                for k in range(8):
                    r = e.matmul(ps_y[:, :], xpT[:, 128 * k:128 * (k + 1)],
                                 w_branch[:, 1024 * k + 512 * cc:1024 * k + 512 * (cc + 1)],
                                 start=(k == 0), stop=(k == 7))
                return r
            tm = P.emit("tensor", mm_y, wait=[t_xpT, t_consts_g, P.tok("eV")], inc="eT")
            t_ymm.append(P.emit("vector", lambda e, cc=cc: e.tensor_copy(
                y_sb[:, 512 * cc:512 * (cc + 1)], ps_y[:, :]), wait=[tm], inc="eV"))
        t_acc = []
        for i in range(4):
            tt = P.emit("vector", lambda e, t=t, i=i, zt=zt: e.tensor_scalar(
                out_t[:, 1024 * i:1024 * (i + 1)], zt[:, 0:1024],
                H[:, 16 * t + 4 * i:16 * t + 4 * i + 1], None, op0=ALU.mult),
                wait=[t_H, P.tok("dS")], inc="eV")
            for jj in range(1, 4):
                tt = P.emit("vector", lambda e, t=t, i=i, jj=jj, zt=zt: e.scalar_tensor_tensor(
                    out_t[:, 1024 * i:1024 * (i + 1)], zt[:, 1024 * jj:1024 * (jj + 1)],
                    H[:, 16 * t + 4 * i + jj:16 * t + 4 * i + jj + 1],
                    out_t[:, 1024 * i:1024 * (i + 1)], op0=ALU.mult, op1=ALU.add),
                    wait=[tt], inc="eV")
            tt = P.emit("vector", lambda e, t=t, i=i: e.scalar_tensor_tensor(
                out_t[:, 1024 * i:1024 * (i + 1)], y_sb[:, :],
                Hpost4[:, 4 * t + i:4 * t + i + 1],
                out_t[:, 1024 * i:1024 * (i + 1)], op0=ALU.mult, op1=ALU.add),
                wait=[tt, t_hpost] + t_ymm, inc="eV")
            t_acc.append(tt)
        t_x = P.dma("sync", x_tile[:, :], x_in[128 * t:128 * (t + 1), :], wait=[P.tok("eV")])
        aux2 = {"colbuf": colbuf[t], "x_ready": t_acc}
        emit_layernorm(P, out_t[:, :], out_t[:, :], D, tmp_bf[:, :], aux2)
        t_add = P.emit("vector", lambda e, t=t: e.tensor_add(out_t[:, :], out_t[:, :], x_tile[:, :]),
                       wait=[aux2["t_out"], P.tok("dS")], inc="eV")
        P.dma("sync", x_out[128 * t:128 * (t + 1), :], out_t[:, :], wait=[t_add])
    P.emit("sync", lambda e: e.nop(), wait=[P.tok("dS")])
    P.flush("s2")


def _np_wcat(L0_ffn_W_res, L0_ffn_W_pre, L0_ffn_W_post):
    return np.ascontiguousarray(
        np.concatenate([L0_ffn_W_res.T, L0_ffn_W_pre.T, L0_ffn_W_post.T], axis=1))


def _np_bases(H_res_base, H_pre_base, H_post_base):
    return np.concatenate([np.asarray(H_res_base).reshape(-1),
                           np.asarray(H_pre_base).reshape(-1),
                           np.asarray(H_post_base).reshape(-1)])[None, :].astype(np.float32)


IDENT_BF = np.eye(128, dtype=bf16)
IDENT_F32 = np.eye(128, dtype=np.float32)
EYE4_ROW = np.eye(4, dtype=np.float32).reshape(1, 16)


def build_nc_stage2():
    nc = bass.Bass("TRN2", target_bir_lowering=False)
    io = {}
    io["x_in"] = nc.dram_tensor("x_in", [TPC, D], F32, kind="ExternalInput").ap()
    io["w_cat"] = nc.dram_tensor("w_cat", [D, 24], F32, kind="ExternalInput").ap()
    io["bases"] = nc.dram_tensor("bases", [1, 24], F32, kind="ExternalInput").ap()
    io["gate"] = nc.dram_tensor("gate", [1, 1], F32, kind="ExternalInput").ap()
    io["w_branch"] = nc.dram_tensor("w_branch", [1024, 1024], F32, kind="ExternalInput").ap()
    io["eye_row"] = nc.dram_tensor("eye_row", [1, 16], F32, kind="ExternalInput").ap()
    io["x_out"] = nc.dram_tensor("x_out", [TPC, D], F32, kind="ExternalOutput").ap()
    io["ident"] = nc.inline_tensor(IDENT_BF, "identc").ap()
    with ExitStack() as stack:
        P = Prog(nc, stack)
        build_stage2(P, io)
    return nc


def run_stage2(x1_full, L0_ffn_W_res, L0_ffn_W_pre, L0_ffn_W_post,
               L0_ffn_H_res_base, L0_ffn_H_pre_base, L0_ffn_H_post_base,
               L0_ffn_gate, L0_ffn_W_branch):
    nc = build_nc_stage2()
    shards = shard_tokens(np.asarray(x1_full, dtype=np.float32))
    wcat = _np_wcat(np.asarray(L0_ffn_W_res), np.asarray(L0_ffn_W_pre), np.asarray(L0_ffn_W_post))
    bases = _np_bases(L0_ffn_H_res_base, L0_ffn_H_pre_base, L0_ffn_H_post_base)
    gate = np.asarray(L0_ffn_gate, dtype=np.float32).reshape(1, 1)
    wbr = np.ascontiguousarray(np.asarray(L0_ffn_W_branch, dtype=np.float32))
    in_maps = [{
        "x_in": s, "w_cat": wcat, "bases": bases, "gate": gate,
        "w_branch": wbr, "eye_row": EYE4_ROW,
    } for s in shards]
    res = run_bass_kernel_spmd(nc, in_maps, core_ids=list(range(NCORES)))
    return unshard_tokens([r["x_out"] for r in res.results])



def _rope_tables(core_id):
    pos = 512 * core_id + np.arange(512, dtype=np.float64)
    inv = 1.0 / (10000.0 ** (np.arange(0, 128, 2, dtype=np.float64) / 128.0))
    ang = np.outer(inv, pos)
    cs = np.repeat(np.cos(ang), 2, axis=0)
    sn = np.repeat(np.sin(ang), 2, axis=0)
    sn[0::2, :] *= -1.0
    return cs.astype(np.float32), sn.astype(np.float32)


def _wq_images(w_qkv):
    w = np.asarray(w_qkv, dtype=np.float32).reshape(32, 128, 48, 128)
    return np.ascontiguousarray(w.transpose(2, 1, 0, 3).reshape(48, 128, 4096))


def _wo_images(w_o):
    w = np.asarray(w_o, dtype=np.float32).reshape(32, 128, 8, 512)
    return np.ascontiguousarray(w.transpose(2, 1, 0, 3).reshape(8, 128, 16384))


SWAP_MASK = [i ^ 1 for i in range(32)]


def build_stage1(P, io):
    nc = P.nc
    st = P.stack
    sb = lambda name, shape, dt=F32: st.enter_context(nc.sbuf_tensor(name, shape, dt))
    x_in, x_out = io["x_in"], io["x_out"]
    snd, rcv = io["snd"], io["rcv"]
    snd2, rcv2 = io["snd2"], io["rcv2"]
    from concourse.aot_env import allow_ici_all_to_all

    identf = sb("s1_identf", [128, 128], F32)
    identb = sb("s1_identb", [128, 128], BF16)
    t_idf = P.dma("sync", identf[:, :], io["identf"][:, :])
    t_idb = P.dma("sync", identb[:, :], io["identb"][:, :])
    colbuf = sb("s1_col", [128, 8])

    with nc.psum_tensor("s1_psa0", [128, 512], F32) as ps_a0, \
         nc.psum_tensor("s1_psa1", [128, 512], F32) as ps_a1, \
         nc.sbuf_tensor("s1_lnT", [128, 64 * 512], BF16) as lnT, \
         nc.sbuf_tensor("s1_wq0", [128, 4096], BF16) as wq0, \
         nc.sbuf_tensor("s1_wq1", [128, 4096], BF16) as wq1, \
         nc.sbuf_tensor("s1_cs", [128, 512], BF16) as cs_b, \
         nc.sbuf_tensor("s1_sn", [128, 512], BF16) as sn_b, \
         nc.sbuf_tensor("s1_ro", [128, 512], BF16) as ro_bf, \
         nc.sbuf_tensor("s1_sw", [128, 512], BF16) as sw_bf, \
         nc.sbuf_tensor("s1_t2", [128, 512], BF16) as t2_bf, \
         nc.sbuf_tensor("s1_x", [128, D], F32) as x_tile, \
         nc.sbuf_tensor("s1_z", [128, D], F32) as z_tile, \
         nc.sbuf_tensor("s1_tmp", [128, D], BF16) as tmp_bf:
        P.dma("gpsimd", cs_b[:, :], io["ropecs"][:, :])
        P.dma("gpsimd", sn_b[:, :], io["ropesn"][:, :])
        t_rope_c = P.tok("dG")
        ps_tr = [ps_a0, ps_a1]
        for g in range(2):
            for tt in range(4):
                gt = 4 * g + tt
                t_x = P.dma("sync", x_tile[:, :], x_in[128 * gt:128 * (gt + 1), :],
                            wait=[P.tok("eV"), P.tok("eS")])
                aux = {"colbuf": colbuf, "x_ready": [P.tok("dS"), P.tok("eV"), P.tok("eS")]}
                emit_layernorm(P, x_tile[:, :], z_tile[:, :], D, tmp_bf[:, :], aux)
                evac_prev = [P.tok("eV"), P.tok("eV")]
                for dch in range(32):
                    ps = ps_tr[dch % 2]
                    t_tr = P.emit("tensor", lambda e, dch=dch, ps=ps: e.transpose(
                        ps[:, 0:128], z_tile[:, 128 * dch:128 * (dch + 1)], identf[:, :]),
                        wait=[aux["t_out"], t_idf, evac_prev[dch % 2]], inc="eT")
                    t_ev = P.emit("vector", lambda e, g=g, dch=dch, tt=tt, ps=ps: e.tensor_copy(
                        lnT[:, 512 * (32 * g + dch) + 128 * tt:512 * (32 * g + dch) + 128 * (tt + 1)],
                        ps[:, 0:128]),
                        wait=[t_tr], inc="eV")
                    evac_prev[dch % 2] = t_ev
        t_lnT = P.tok("eV")
        wqb = [wq0, wq1]
        for cc in range(48):
            wq = wqb[cc % 2]
            t_w = P.dma("gpsimd", wq[:, :], io["wq_img"][cc].squeeze(0) if False else
                        io["wq_img"][128 * cc:128 * (cc + 1), :],
                        wait=[P.tok("eT")])
            if cc < 32:
                row = 768 * (cc // 4) + 128 * (cc % 4)
            elif cc < 40:
                row = 768 * (cc - 32) + 512
            else:
                row = 768 * (cc - 40) + 640
            for g in range(2):
                ps = ps_tr[g]
                def mm_qkv(e, cc=cc, g=g, ps=ps, wq=wq):
                    r = None
                    for dch in range(32):
                        r = e.matmul(ps[:, :], wq[:, 128 * dch:128 * (dch + 1)],
                                     lnT[:, 512 * (32 * g + dch):512 * (32 * g + dch + 1)],
                                     start=(dch == 0), stop=(dch == 31))
                    return r
                t_mm = P.emit("tensor", mm_qkv,
                              wait=[P.tok("dG"), t_lnT, P.tok("eS"), P.tok("eV")], inc="eT")
                t_cp = P.emit("scalar", lambda e, ps=ps: e.copy(ro_bf[:, :], ps[:, :]),
                              wait=[t_mm, P.tok("dG"), P.tok("eV")], inc="eS")
                if cc < 40:
                    t_sw = P.emit("vector", lambda e: e.stream_shuffle(sw_bf[:, :], ro_bf[:, :],
                                                                       SWAP_MASK),
                                  wait=[t_cp, t_rope_c], inc="eV")
                    t_r1 = P.emit("vector", lambda e: e.tensor_mul(ro_bf[:, :], ro_bf[:, :],
                                                                   cs_b[:, :]),
                                  wait=[t_sw], inc="eV")
                    t_r2 = P.emit("vector", lambda e: e.tensor_mul(t2_bf[:, :], sw_bf[:, :],
                                                                   sn_b[:, :]),
                                  wait=[t_sw], inc="eV")
                    t_ro = P.emit("vector", lambda e: e.tensor_add(ro_bf[:, :], ro_bf[:, :],
                                                                   t2_bf[:, :]),
                                  wait=[t_r1, t_r2], inc="eV")
                else:
                    t_ro = t_cp
                P.dma("gpsimd", snd.ap()[row:row + 128, 512 * g:512 * (g + 1)], ro_bf[:, :],
                      wait=[t_ro])
        t_a2a1 = P.emit("gpsimd", lambda e: _cc_a2a(e, snd, rcv),
                        wait=[P.tok("dG"), P.tok("dS")], inc="cc2")
    P.barrier()
    P.flush("s1_A")

    with nc.psum_tensor("s1_pss", [128, 512], F32) as ps_s, \
         nc.psum_tensor("s1_pv0", [128, 132], F32) as pv0, \
         nc.psum_tensor("s1_pv1", [128, 132], F32) as pv1, \
         nc.psum_tensor("s1_pv2", [128, 132], F32) as pv2, \
         nc.psum_tensor("s1_pv3", [128, 132], F32) as pv3, \
         nc.psum_tensor("s1_ptr", [128, 128], BF16) as ps_tr2, \
         nc.sbuf_tensor("s1_kT", [128, 32 * 128], BF16) as kT, \
         nc.sbuf_tensor("s1_vT", [128, 32 * 128], BF16) as vT, \
         nc.sbuf_tensor("s1_V", [128, 32 * 132], BF16) as V_all, \
         nc.sbuf_tensor("s1_qT", [128, 32 * 128], BF16) as qT, \
         nc.sbuf_tensor("s1_pb", [128, 512], BF16) as probs, \
         nc.sbuf_tensor("s1_ab", [128, 128], BF16) as attn_bf, \
         nc.sbuf_tensor("s1_aT", [128, 512], BF16) as attnT, \
         nc.sbuf_tensor("s1_dn", [128, 4], F32) as dnc, \
         nc.sbuf_tensor("s1_idb2", [128, 128], BF16) as identb2:
        pvs = [pv0, pv1, pv2, pv3]
        t_idb2 = P.emit("vector", lambda e: e.tensor_copy(identb2[:, :], identb[:, :]),
                        wait=[t_idb], inc="eV")
        t_ones = P.emit("vector", lambda e: e.memset(
            V_all[:, :].rearrange("p (c w) -> p c w", w=132)[:, :, 128:129], 1.0), inc="eV")
        m20 = P.konst(-20.0)
        for b in range(2):
            t_kv = []
            for r in range(8):
                t_kv.append(P.dma("sync", kT[:, 512 * r:512 * (r + 1)],
                                  rcv.ap()[768 * r + 512:768 * r + 640, 512 * b:512 * (b + 1)],
                                  wait=[("cc2", 1), P.tok("eT"), P.tok("eV")]))
                t_kv.append(P.dma("sync", vT[:, 512 * r:512 * (r + 1)],
                                  rcv.ap()[768 * r + 640:768 * r + 768, 512 * b:512 * (b + 1)],
                                  wait=[("cc2", 1)]))
            t_kv = [P.tok("dS")]
            t_vta = []
            for jt in range(32):
                t_vt = P.emit("tensor", lambda e, jt=jt: e.transpose(
                    ps_tr2[:, :], vT[:, 128 * jt:128 * (jt + 1)], identb2[:, :]),
                    wait=t_kv + [t_idb2, P.tok("eV")], inc="eT")
                t_vta.append(P.emit("vector", lambda e, jt=jt: e.tensor_copy(
                    V_all[:, 132 * jt:132 * jt + 128], ps_tr2[:, :]),
                    wait=[t_vt], inc="eV"))
            t_vready = P.tok("eV")
            for h in range(4):
                for r in range(8):
                    P.dma("sync", qT[:, 512 * r:512 * (r + 1)],
                          rcv.ap()[768 * r + 128 * h:768 * r + 128 * (h + 1),
                                   512 * b:512 * (b + 1)],
                          wait=[("cc2", 1), P.tok("eT")])
                t_q = P.tok("dS")
                for qc in range(8):
                    for jt in range(4 * qc + 4):
                        t_sc = P.emit("tensor", lambda e, jt=jt, qc=qc: e.matmul(
                            ps_s[:, :], kT[:, 128 * jt:128 * (jt + 1)],
                            qT[:, 512 * qc:512 * (qc + 1)], start=True, stop=True),
                            wait=[t_q, t_vready, P.tok("eS"), P.tok("eG")], inc="eT")
                        t_ex = P.emit("scalar", lambda e: e.activation(
                            out=probs[:, :], in_=ps_s[:, :], func=AF.Exp,
                            scale=float(1.0 / math.sqrt(128.0)), bias=m20),
                            wait=[t_sc, P.ktok, P.tok("eG"), P.tok("eT")], inc="eS")
                        t_pb = t_ex
                        m = jt - 4 * qc
                        if m >= 0:
                            t_pb = P.emit("gpsimd", lambda e, m=m: e.affine_select(
                                probs[:, :], probs[:, :], pattern=[[1, 512]],
                                compare_op=ALU.is_ge, fill=0.0, base=-128 * m,
                                channel_multiplier=-1),
                                wait=[t_ex], inc="eG")
                        for qt in range(max(m, 0), 4):
                            w_pv = [t_pb, t_ones] + ([P.tok("eV")] if jt == 0 else [])
                            P.emit("tensor", lambda e, jt=jt, qt=qt, qc=qc: e.matmul(
                                pvs[qt][:, 0:129], probs[:, 128 * qt:128 * (qt + 1)],
                                V_all[:, 132 * jt:132 * jt + 129],
                                start=(jt == 0), stop=(jt == 4 * qc + qt)),
                                wait=w_pv, inc="eT")
                    t_pv = P.tok("eT")
                    tr_prev = P.tok("eV")
                    t_dg_guard = P.tok("dG")
                    for qt in range(4):
                        t_rc = P.emit("vector", lambda e, qt=qt: e.reciprocal(
                            dnc[:, qt:qt + 1], pvs[qt][:, 128:129]),
                            wait=[t_pv], inc="eV")
                        t_sc2 = P.emit("vector", lambda e, qt=qt: e.tensor_scalar(
                            attn_bf[:, :], pvs[qt][:, 0:128], dnc[:, qt:qt + 1], None,
                            op0=ALU.mult),
                            wait=[t_rc], inc="eV")
                        t_tr = P.emit("tensor", lambda e, qt=qt: e.transpose(
                            ps_tr2[:, :], attn_bf[:, :], identb2[:, :]),
                            wait=[t_sc2, tr_prev], inc="eT")
                        tr_prev = P.emit("vector", lambda e, qt=qt: e.tensor_copy(
                            attnT[:, 128 * qt:128 * (qt + 1)], ps_tr2[:, :]),
                            wait=[t_tr, t_dg_guard], inc="eV")
                    P.dma("gpsimd",
                          snd2.ap()[512 * qc + 128 * h:512 * qc + 128 * (h + 1),
                                    512 * b:512 * (b + 1)],
                          attnT[:, :], wait=[tr_prev])
        t_a2a2 = P.emit("gpsimd", lambda e: _cc_a2a(e, snd2, rcv2),
                        wait=[P.tok("dG"), P.tok("dS")], inc="cc3")
    P.barrier()
    P.flush("s1_C")

    with nc.psum_tensor("s1_pso0", [128, 512], F32) as ps_o0, \
         nc.psum_tensor("s1_pso1", [128, 512], F32) as ps_o1, \
         nc.sbuf_tensor("s1_asb", [128, 32 * 1024], BF16) as attn_sb, \
         nc.sbuf_tensor("s1_wo", [128, 16384], BF16) as wob, \
         nc.sbuf_tensor("s1_ob", [128, 8 * 4096], BF16) as out_bf, \
         nc.sbuf_tensor("s1_y", [128, D], F32) as y_t, \
         nc.sbuf_tensor("s1_x2", [128, D], F32) as x_t2, \
         nc.sbuf_tensor("s1_tmp2", [128, D], BF16) as tmp2:
        for g in range(32):
            P.dma("sync", attn_sb[:, 1024 * g:1024 * (g + 1)],
                  rcv2.ap()[128 * g:128 * (g + 1), :], wait=[("cc3", 1)])
        t_asb = P.tok("dS")
        psos = [ps_o0, ps_o1]
        for dcc in range(8):
            t_w1 = P.dma("gpsimd", wob[:, 0:8192],
                         io["wo_img"][128 * dcc:128 * (dcc + 1), 0:8192],
                         wait=[P.tok("eT")])
            t_w2 = P.dma("gpsimd", wob[:, 8192:16384],
                         io["wo_img"][128 * dcc:128 * (dcc + 1), 8192:16384],
                         wait=[P.tok("eT")])
            t_wo = P.tok("dG")
            for tt in range(8):
                bb, lt = tt // 4, tt % 4
                ps = psos[tt % 2]
                def mm_wo(e, bb=bb, lt=lt, ps=ps):
                    r = None
                    for g in range(32):
                        r = e.matmul(ps[:, :],
                                     attn_sb[:, 1024 * g + 512 * bb + 128 * lt:
                                             1024 * g + 512 * bb + 128 * (lt + 1)],
                                     wob[:, 512 * g:512 * (g + 1)],
                                     start=(g == 0), stop=(g == 31))
                    return r
                t_mm = P.emit("tensor", mm_wo,
                              wait=[t_wo, t_asb, P.tok("eV")], inc="eT")
                P.emit("vector", lambda e, tt=tt, dcc=dcc, ps=ps: e.tensor_copy(
                    out_bf[:, 4096 * tt + 512 * dcc:4096 * tt + 512 * (dcc + 1)], ps[:, :]),
                    wait=[t_mm], inc="eV")
        t_ob = P.tok("eV")
        for tt in range(8):
            t_x = P.dma("sync", x_t2[:, :], x_in[128 * tt:128 * (tt + 1), :],
                        wait=[P.tok("eV")])
            aux = {"colbuf": colbuf,
                   "x_ready": [t_ob, P.tok("eS"), P.tok("eV")]}
            emit_layernorm(P, out_bf[:, 4096 * tt:4096 * (tt + 1)], y_t[:, :], D,
                           tmp2[:, :], aux)
            t_add = P.emit("vector", lambda e: e.tensor_add(y_t[:, :], y_t[:, :], x_t2[:, :]),
                           wait=[aux["t_out"], P.tok("dS")], inc="eV")
            P.dma("sync", x_out[128 * tt:128 * (tt + 1), :], y_t[:, :], wait=[t_add])
        P.emit("sync", lambda e: e.nop(), wait=[P.tok("dS")])
        P.flush("s1_E")


def _cc_a2a(e, snd, rcv):
    from concourse.aot_env import allow_ici_all_to_all
    with allow_ici_all_to_all():
        return e.collective_compute(
            "AllToAll", ALU.bypass, replica_groups=[list(range(8))],
            ins=[snd.ap()], outs=[rcv.ap()])


def build_nc_stage1():
    nc = bass.Bass("TRN2", target_bir_lowering=False)
    io = {}
    io["x_in"] = nc.dram_tensor("x_in", [TPC, D], F32, kind="ExternalInput").ap()
    io["wq_img"] = nc.dram_tensor("wq_img", [48 * 128, 4096], F32, kind="ExternalInput").ap()
    io["wo_img"] = nc.dram_tensor("wo_img", [8 * 128, 16384], F32, kind="ExternalInput").ap()
    io["ropecs"] = nc.dram_tensor("ropecs", [128, 512], F32, kind="ExternalInput").ap()
    io["ropesn"] = nc.dram_tensor("ropesn", [128, 512], F32, kind="ExternalInput").ap()
    io["x_out"] = nc.dram_tensor("x_out", [TPC, D], F32, kind="ExternalOutput").ap()
    io["identf"] = nc.inline_tensor(IDENT_F32, "identf").ap()
    io["identb"] = nc.inline_tensor(IDENT_BF, "identb").ap()
    io["snd"] = nc.dram_tensor("snd", [6144, 1024], BF16, kind="Internal")
    io["rcv"] = nc.dram_tensor("rcv", [6144, 1024], BF16, kind="Internal")
    io["snd2"] = nc.dram_tensor("snd2", [4096, 1024], BF16, kind="Internal")
    io["rcv2"] = nc.dram_tensor("rcv2", [4096, 1024], BF16, kind="Internal")
    with ExitStack() as stack:
        P = Prog(nc, stack)
        build_stage1(P, io)
    return nc


def run_stage1(x_full, w_qkv, w_o):
    nc = build_nc_stage1()
    shards = shard_tokens(np.asarray(x_full, dtype=np.float32))
    wq_img = _wq_images(w_qkv).reshape(48 * 128, 4096)
    wo_img = _wo_images(w_o).reshape(8 * 128, 16384)
    in_maps = []
    for i in range(NCORES):
        cs, sn = _rope_tables(i)
        in_maps.append({"x_in": shards[i], "wq_img": wq_img, "wo_img": wo_img,
                        "ropecs": cs, "ropesn": sn})
    res = run_bass_kernel_spmd(nc, in_maps, core_ids=list(range(NCORES)))
    return unshard_tokens([r["x_out"] for r in res.results])



FCH = 17
NF = FCH * 128
NLAG = 2176


def _s3_consts():
    L = 4096
    t = np.arange(L, dtype=np.float64)
    f = np.arange(NF, dtype=np.float64)
    ang = np.outer(t, f) * (2 * np.pi / L)
    fre = np.cos(ang)
    fim = -np.sin(ang)
    fre[:, 2049:] = 0.0
    fim[:, 2049:] = 0.0
    def img(M):
        out = np.zeros((FCH * 128, L), dtype=np.float64)
        for fc in range(FCH):
            blk = M[:, fc * 128:(fc + 1) * 128]
            b = blk.reshape(32, 128, 128)
            out[fc * 128:(fc + 1) * 128, :] = b.transpose(1, 0, 2).reshape(128, L)
        return out
    fre_i = img(fre).astype(bf16)
    fim_i = img(fim).astype(bf16)
    lag = np.arange(NLAG, dtype=np.float64)
    wt = np.full(NF, 2.0 / L)
    wt[0] = 1.0 / L
    wt[2048] = 1.0 / L
    wt[2049:] = 0.0
    cinv = wt[:, None] * np.cos(np.outer(f, lag) * (2 * np.pi / L))
    cinv[2049:, :] = 0.0
    cinv[:, 2049:] = 0.0
    cinv_i = cinv.astype(bf16)
    return fre_i, fim_i, cinv_i


def build_stage3(P, io):
    nc = P.nc
    st = P.stack
    sb = lambda name, shape, dt=F32: st.enter_context(nc.sbuf_tensor(name, shape, dt))
    x_in, x_out = io["x_in"], io["x_out"]
    dbl = io["dbl"].ap()

    identf = sb("s3_identf", [128, 128], F32)
    t_idf = P.dma("sync", identf[:, :], io["identf"][:, :])
    colbuf = sb("s3_col", [128, 8])
    P_sb = sb("s3_P", [128, FCH * TPC], BF16)
    ps_re = st.enter_context(nc.psum_tensor("s3_psre", [128, 512], F32))
    ps_im = st.enter_context(nc.psum_tensor("s3_psim", [128, 512], F32))
    ps_c = [st.enter_context(nc.psum_tensor(f"s3_psc{i}", [128, 512], F32)) for i in range(5)]

    with nc.sbuf_tensor("s3_lnT", [128, 32 * 512], BF16) as lnT, \
         nc.sbuf_tensor("s3_f1", [128, D], BF16) as fbuf1, \
         nc.sbuf_tensor("s3_f2", [128, D], BF16) as fbuf2, \
         nc.sbuf_tensor("s3_x", [128, D], F32) as x_tile, \
         nc.sbuf_tensor("s3_z", [128, D], F32) as z_tile, \
         nc.sbuf_tensor("s3_tmp", [128, D], BF16) as tmp_bf, \
         nc.sbuf_tensor("s3_sq", [128, 512], F32) as sq_t:
        for g in range(2):
            for tt in range(4):
                gt = 4 * g + tt
                t_x = P.dma("sync", x_tile[:, :], x_in[128 * gt:128 * (gt + 1), :],
                            wait=[P.tok("eV"), P.tok("eS")])
                aux = {"colbuf": colbuf, "x_ready": [P.tok("dS"), P.tok("eV"), P.tok("eS")]}
                emit_layernorm(P, x_tile[:, :], z_tile[:, :], D, tmp_bf[:, :], aux)
                P.dma("gpsimd", dbl[128 * gt:128 * (gt + 1), 0:D], z_tile[:, :],
                      wait=[aux["t_out"]])
                P.dma("gpsimd", dbl[128 * gt:128 * (gt + 1), D:2 * D], z_tile[:, :],
                      wait=[aux["t_out"]])
                import os as _os
                if _os.environ.get("S3_SKIP_TR", "0") == "1":
                    continue
                evac_prev = [P.tok("eV"), P.tok("eV")]
                for dch in range(32):
                    ps = ps_re if dch % 2 == 0 else ps_im
                    t_tr = P.emit("tensor", lambda e, dch=dch, ps=ps: e.transpose(
                        ps[:, 0:128], z_tile[:, 128 * dch:128 * (dch + 1)], identf[:, :]),
                        wait=[aux["t_out"], t_idf, evac_prev[dch % 2]], inc="eT")
                    t_ev = P.emit("vector", lambda e, dch=dch, tt=tt, ps=ps: e.tensor_copy(
                        lnT[:, 512 * dch + 128 * tt:512 * dch + 128 * (tt + 1)], ps[:, 0:128]),
                        wait=[t_tr], inc="eV")
                    evac_prev[dch % 2] = t_ev
            t_lnT = P.tok("eV")
            if _os.environ.get("S3_SKIP_DFT", "0") == "1":
                continue
            fb = [fbuf1, fbuf2]
            t_fload = [P.tok("dS"), P.tok("dS")]
            for fc in range(FCH):
                fre_src = io["fre"][128 * fc:128 * (fc + 1), :]
                fim_src = io["fim"][128 * fc:128 * (fc + 1), :]
                b = fc % 2
                t_l1 = P.dma("sync", fbuf1[:, :], fre_src, wait=[P.tok("eT")])
                t_l2 = P.dma("sync", fbuf2[:, :], fim_src, wait=[P.tok("eT")])
                t_w = P.tok("dS")

                def mm_dft(e, ps, fbuf, fc=fc):
                    r = None
                    for tc in range(32):
                        r = e.matmul(ps[:, :], fbuf[:, 128 * tc:128 * (tc + 1)],
                                     lnT[:, 512 * tc:512 * (tc + 1)],
                                     start=(tc == 0), stop=(tc == 31))
                    return r
                if _os.environ.get("S3_DFT_LOADONLY", "0") == "1":
                    continue
                t_mre = P.emit("tensor", lambda e, f=mm_dft: f(e, ps_re, fbuf1),
                               wait=[t_w, t_lnT, P.tok("eS"), P.tok("eV")], inc="eT")
                if _os.environ.get("S3_DFT_ONEMM", "0") == "1":
                    t_cp = P.emit("vector", lambda e, fc=fc, g=g: e.tensor_copy(
                        P_sb[:, fc * TPC + 512 * g:fc * TPC + 512 * (g + 1)], ps_re[:, :]),
                        wait=[t_mre], inc="eV")
                    continue
                t_mim = P.emit("tensor", lambda e, f=mm_dft: f(e, ps_im, fbuf2),
                               wait=[t_w, t_lnT, P.tok("eV")], inc="eT")
                t_s1 = P.emit("scalar", lambda e: e.activation(out=sq_t[:, :], in_=ps_re[:, :],
                                                               func=AF.Square),
                              wait=[t_mre], inc="eS")
                t_s2 = P.emit("scalar", lambda e: e.activation(out=ps_re[:, :], in_=ps_im[:, :],
                                                               func=AF.Square),
                              wait=[t_mim, t_s1], inc="eS")
                t_p = P.emit("vector", lambda e, fc=fc, g=g: e.tensor_tensor(
                    P_sb[:, fc * TPC + 512 * g:fc * TPC + 512 * (g + 1)],
                    sq_t[:, :], ps_re[:, :], ALU.add),
                    wait=[t_s2], inc="eV")
    P.barrier()
    P.flush("s3_A")

    import os as _os
    if _os.environ.get("S3_PHASEA_ONLY", "0") == "1":
        with nc.sbuf_tensor("s3_pass", [128, D], F32) as pt:
            for gt in range(NT):
                t_x = P.dma("sync", pt[:, :], x_in[128 * gt:128 * (gt + 1), :],
                            wait=[P.tok("dS")])
                P.dma("sync", x_out[128 * gt:128 * (gt + 1), :], pt[:, :],
                      wait=[P.tok("dS")])
            P.emit("sync", lambda e: e.nop(), wait=[P.tok("dS")])
            P.flush("s3_stub")
        return

    with nc.sbuf_tensor("s3_cinv", [128, FCH * NLAG], BF16) as cinv_sb, \
         nc.sbuf_tensor("s3_ch", [128, NLAG], F32) as corr_h, \
         nc.sbuf_tensor("s3_cf", [128, 4096], F32) as corr_f, \
         nc.sbuf_tensor("s3_g0", [128, D], F32) as g0, \
         nc.sbuf_tensor("s3_g1", [128, D], F32) as g1, \
         nc.sbuf_tensor("s3_y", [128, D], F32) as y_sb, \
         nc.sbuf_tensor("s3_x2", [128, D], F32) as x_tile2, \
         nc.sbuf_tensor("s3_mx", [128, 8], F32) as mx, \
         nc.sbuf_tensor("s3_ix", [128, 8], U32) as ix, \
         nc.sbuf_tensor("s3_wt", [128, 8], F32) as wt, \
         nc.sbuf_tensor("s3_sc", [128, 8], F32) as sc, \
         nc.sbuf_tensor("s3_idx", [128, 8], mybir.dt.int32) as idxi, \
         nc.sbuf_tensor("s3_wt2", [128, 8], F32) as wt2, \
         nc.sbuf_tensor("s3_iota", [128, 1], F32) as iota_c:
        for fc in range(FCH):
            P.dma("sync", cinv_sb[:, NLAG * fc:NLAG * (fc + 1)],
                  io["cinv"][128 * fc:128 * (fc + 1), :])
        t_cinv = P.tok("dS")
        gathers = [g0, g1]
        for gt in range(NT):
            def mm_corr(e, gt=gt):
                r = None
                g = gt // 4
                q = gt % 4
                for fc in range(FCH):
                    lh = P_sb[:, fc * TPC + 512 * g + 128 * q:fc * TPC + 512 * g + 128 * (q + 1)]
                    for tc in range(5):
                        wdt = 512 if tc < 4 else 128
                        r = e.matmul(ps_c[tc][:, 0:wdt], lh,
                                     cinv_sb[:, NLAG * fc + 512 * tc:NLAG * fc + 512 * tc + wdt],
                                     start=(fc == 0), stop=(fc == FCH - 1))
                return r
            t_cm = P.emit("tensor", mm_corr,
                          wait=[t_cinv, P.tok("eV"), P.tok("eS")], inc="eT")
            t_ce = []
            for tc in range(5):
                wdt = 512 if tc < 4 else 128
                t_ce.append(P.emit("vector", lambda e, tc=tc, wdt=wdt: e.tensor_copy(
                    corr_h[:, 512 * tc:512 * tc + wdt], ps_c[tc][:, 0:wdt]),
                    wait=[t_cm], inc="eV"))
            t_cf1 = P.emit("vector", lambda e: e.tensor_copy(corr_f[:, 0:2048],
                                                             corr_h[:, 0:2048]),
                           wait=t_ce, inc="eV")
            import os as _os
            if _os.environ.get("S3_NOREV", "0") == "1":
                t_cf2 = P.emit("vector", lambda e: e.tensor_copy(
                    corr_f[:, 2048:4096], corr_h[:, 0:2048]),
                    wait=t_ce, inc="eV")
            else:
                t_cf2 = P.emit("vector", lambda e: e.tensor_copy(
                    corr_f[:, 2048:4096], corr_h[:, 2048:0:-1]),
                    wait=t_ce, inc="eV")
            t_m = P.emit("vector", lambda e: e.max(out=mx[:, :], in_=corr_f[:, :]),
                         wait=[t_cf1, t_cf2], inc="eV")
            t_i = P.emit("vector", lambda e: e.max_index(out=ix[:, :], in_max=mx[:, :],
                                                         in_values=corr_f[:, :]),
                         wait=[t_m], inc="eV")
            t_n = P.emit("vector", lambda e: e.tensor_scalar_mul(sc[:, 0:1], mx[:, 0:1], -1.0),
                         wait=[t_m], inc="eV")
            t_e = P.emit("scalar", lambda e: e.activation(out=wt[:, :], in_=mx[:, :], func=AF.Exp,
                                                          bias=sc[:, 0:1],
                                                          accum_out=sc[:, 1:2]),
                         wait=[t_n], inc="eS")
            t_r = P.emit("vector", lambda e: e.reciprocal(sc[:, 2:3], sc[:, 1:2]),
                         wait=[t_e], inc="eV")
            t_wt = P.emit("vector", lambda e: e.tensor_scalar(wt[:, :], wt[:, :], sc[:, 2:3],
                                                              None, op0=ALU.mult),
                          wait=[t_r], inc="eV")
            t_io = P.emit("gpsimd", lambda e, gt=gt: e.iota(
                iota_c[:, :], pattern=[[0, 1]], base=128 * gt * 8192,
                channel_multiplier=8192, allow_small_or_imprecise_dtypes=True),
                wait=[P.tok("dG")], inc="eG")
            t_ii = P.emit("vector", lambda e: e.tensor_copy(sc[:, 3:4].to_broadcast([128, 1]) if False else wt2[:, 0:8], ix[:, :]),
                          wait=[t_i], inc="eV")
            t_ia = P.emit("vector", lambda e: e.tensor_scalar(wt2[:, 0:8], wt2[:, 0:8],
                                                              iota_c[:, 0:1], None, op0=ALU.add),
                          wait=[t_ii, t_io], inc="eV")
            t_ic = P.emit("vector", lambda e: e.tensor_copy(idxi[:, :], wt2[:, 0:8]),
                          wait=[t_ia], inc="eV")
            t_ia = t_ic
            t_prev_tap = None
            for k in range(8):
                gb = gathers[k % 2]
                import os as _os
                if _os.environ.get("S3_NOGATHER", "0") == "1":
                    t_g = P.dma("gpsimd", gb[:, :], dbl[128 * gt:128 * (gt + 1), 0:D],
                                wait=[t_ia, P.tok("eV"), P.tok("eS"), P.tok("dG")])
                else:
                    t_g = P.emit("gpsimd", lambda e, k=k, gb=gb: e.indirect_dma_start(
                        out=gb[:, :], out_offset=None,
                        in_=dbl.rearrange("a b -> (a b)").unsqueeze(1),
                        in_offset=IndirectOffsetOnAxis(idxi[:, k:k + 1], 0)),
                        wait=[t_ia, P.tok("eV"), P.tok("eS"), P.tok("dG")], inc="dG", inc_by=16)
                if k == 0:
                    t_prev_tap = P.emit("vector", lambda e, gb=gb: e.tensor_scalar(
                        y_sb[:, :], gb[:, :], wt[:, 0:1], None, op0=ALU.mult),
                        wait=[t_g, t_wt], inc="eV")
                else:
                    t_prev_tap = P.emit("vector", lambda e, k=k, gb=gb: e.scalar_tensor_tensor(
                        y_sb[:, :], gb[:, :], wt[:, k:k + 1], y_sb[:, :],
                        op0=ALU.mult, op1=ALU.add),
                        wait=[t_g, t_prev_tap], inc="eV")
            t_x = P.dma("sync", x_tile2[:, :], x_in[128 * gt:128 * (gt + 1), :],
                        wait=[P.tok("eV")])
            aux2 = {"colbuf": colbuf,
                    "x_ready": [t_prev_tap, P.tok("eS")]}
            emit_layernorm(P, y_sb[:, :], y_sb[:, :], D, g0[:, :], aux2)
            t_add = P.emit("vector", lambda e: e.tensor_add(y_sb[:, :], y_sb[:, :],
                                                            x_tile2[:, :]),
                           wait=[aux2["t_out"], P.tok("dS")], inc="eV")
            P.dma("sync", x_out[128 * gt:128 * (gt + 1), :], y_sb[:, :], wait=[t_add])
        P.emit("sync", lambda e: e.nop(), wait=[P.tok("dS")])
        P.flush("s3_B")


def build_nc_stage3():
    nc = bass.Bass("TRN2", target_bir_lowering=False)
    fre_i, fim_i, cinv_i = _s3_consts()
    io = {}
    io["x_in"] = nc.dram_tensor("x_in", [TPC, D], F32, kind="ExternalInput").ap()
    io["fre"] = nc.dram_tensor("fre", [FCH * 128, 4096], BF16, kind="ExternalInput").ap()
    io["fim"] = nc.dram_tensor("fim", [FCH * 128, 4096], BF16, kind="ExternalInput").ap()
    io["cinv"] = nc.dram_tensor("cinv", [FCH * 128, NLAG], BF16, kind="ExternalInput").ap()
    io["x_out"] = nc.dram_tensor("x_out", [TPC, D], F32, kind="ExternalOutput").ap()
    io["identf"] = nc.inline_tensor(IDENT_F32, "identf").ap()
    io["dbl"] = nc.dram_tensor("dbl", [TPC, 2 * D], F32, kind="Internal")
    with ExitStack() as stack:
        P = Prog(nc, stack)
        build_stage3(P, io)
        P.flush("s3")
    return nc, (fre_i, fim_i, cinv_i)


def run_stage3(x2_full):
    nc, (fre_i, fim_i, cinv_i) = build_nc_stage3()
    shards = shard_tokens(np.asarray(x2_full, dtype=np.float32))
    in_maps = [{"x_in": s, "fre": fre_i, "fim": fim_i, "cinv": cinv_i} for s in shards]
    res = run_bass_kernel_spmd(nc, in_maps, core_ids=list(range(NCORES)))
    return unshard_tokens([r["x_out"] for r in res.results])



_TRIPLES = [(1, 2, 3), (1, 4, 5), (1, 7, 6), (2, 4, 6), (2, 5, 7), (3, 4, 7), (3, 6, 5)]


def _oct_struct():
    S = np.zeros((8, 8, 8), np.float32)
    S[0, 0, 0] = 1.0
    for i in range(1, 8):
        S[0, i, i] = 1.0
        S[i, 0, i] = 1.0
        S[i, i, 0] = -1.0
    for a, b, c in _TRIPLES:
        for x, y, z in ((a, b, c), (b, c, a), (c, a, b)):
            S[x, y, z] = 1.0
            S[y, x, z] = -1.0
    return S


def _oct_tables():
    S = _oct_struct()
    j_tab = np.zeros((8, 8), np.int64)
    s_tab = np.zeros((8, 8), np.float32)
    for i in range(8):
        for k in range(8):
            js = np.nonzero(S[:, i, k])[0]
            assert len(js) == 1, (i, k, js)
            j_tab[i, k] = js[0]
            s_tab[i, k] = S[js[0], i, k]
    return j_tab, s_tab


J_TAB, S_TAB = _oct_tables()
HID = 10928
OC_N = 11
OCW = [128] * 10 + [86]
OC_BASE = [128 * i for i in range(11)]
GIMG = 699392
WQ_FLAT = 3 * GIMG
N_W_ELEM = 8 * 1366 * 512


def _g_img_off(oc):
    return sum(128 * 4 * OCW[o] for o in range(oc))


def build_stage4(P, io):
    nc = P.nc
    st = P.stack
    sb = lambda name, shape, dt=F32: st.enter_context(nc.sbuf_tensor(name, shape, dt))

    x_in, x_out = io["x_in"], io["x_out"]
    wq_mine = io["wq_mine"]
    wq_all = io["wq_all"]

    ident = sb("s4_ident", [128, 128], BF16)
    identf = sb("s4_identf", [128, 128], F32)
    t_ident = P.dma("sync", ident[:, :], io["ident"][:, :])
    t_identf = P.dma("sync", identf[:, :], io["identf"][:, :])

    with nc.psum_tensor("s4q_pt", [128, 128], BF16) as ps_t, \
         nc.psum_tensor("s4q_pt2", [128, 128], BF16) as ps_t2, \
         nc.psum_tensor("s4q_red", [4, 8], F32) as ps_red, \
         nc.sbuf_tensor("s4q_w", [128, 1366], F32) as wtile, \
         nc.sbuf_tensor("s4q_q", [128, 1366], BF16) as qtile, \
         nc.sbuf_tensor("s4q_q2", [128, 1366], BF16) as qtile2, \
         nc.sbuf_tensor("s4q_t", [128, 128], BF16) as ttile, \
         nc.sbuf_tensor("s4q_acc", [128, 4], F32) as acc, \
         nc.sbuf_tensor("s4q_redsb", [4, 2], F32) as red, \
         nc.sbuf_tensor("s4q_scl", [128, 8], F32) as sclq:
        scl = sclq
        ones = P.konst(1.0)

        shards = [("wg", io["wg"], 11, 512, 0), ("wu", io["wu"], 11, 512, GIMG),
                  ("wd", io["wd"], 4, 1366, 2 * GIMG)]
        t_acc = []
        for mi, (nm, w_ap, ntile, cols, _off) in enumerate(shards):
            tz = P.emit("vector", lambda e, mi=mi: e.memset(acc[:, mi:mi + 1], 0.0), inc="eV")
            tprev = tz
            for ti in range(ntile):
                rows = min(128, w_ap.shape[0] - 128 * ti)
                t_l = P.dma("sync", wtile[0:rows, 0:cols], w_ap[128 * ti:128 * ti + rows, :],
                            wait=[tprev])
                t_r = P.emit("vector", lambda e, rows=rows, cols=cols: e.tensor_reduce(
                    acc[0:rows, 3:4], wtile[0:rows, 0:cols], axis=AX.X, op=ALU.add,
                    apply_absolute_value=True), wait=[P.tok("dS")], inc="eV")
                tprev = P.emit("vector", lambda e, rows=rows, mi=mi: e.tensor_tensor(
                    acc[0:rows, mi:mi + 1], acc[0:rows, mi:mi + 1], acc[0:rows, 3:4], ALU.add),
                    wait=[t_r], inc="eV")
            t_acc.append(tprev)
        t_mm = P.emit("tensor", lambda e: e.matmul(ps_red[0:3, 0:1], acc[:, 0:3], ones,
                                                   start=True, stop=True),
                      wait=t_acc + [P.ktok], inc="eT")
        t_red0 = P.emit("vector", lambda e: e.memset(red[0:4, 0:2], 0.0), inc="eV")
        t_red = P.emit("vector", lambda e: e.tensor_copy(red[0:3, 0:1], ps_red[0:3, 0:1]),
                       wait=[t_mm, t_red0], inc="eV")
        P.dma("sync", io["ar_in"].ap()[0:4, 0:1], red[0:4, 0:1], wait=[t_red])
        t_ar = P.emit("gpsimd", lambda e: e.collective_compute(
            "AllReduce", ALU.add, replica_groups=[list(range(8))],
            ins=[io["ar_in"].ap()], outs=[io["ar_out"].ap()]),
            wait=[P.tok("dS")], inc="cc0")
        t_sl = P.dma("sync", scl[:, 0:4],
                     io["ar_out"].ap().rearrange("a b -> (a b)").unsqueeze(0).to_broadcast([128, 4]),
                     wait=[t_ar])
        t_s1 = P.emit("vector", lambda e: e.tensor_scalar_mul(scl[:, 0:4], scl[:, 0:4],
                                                              1.0 / N_W_ELEM),
                      wait=[P.tok("dS")], inc="eV")
        t_s2 = P.emit("vector", lambda e: e.reciprocal(scl[:, 4:8], scl[:, 0:4]),
                      wait=[t_s1], inc="eV")

        for mi, (nm, w_ap, ntile, cols, off) in enumerate(shards):
            for ti in range(ntile):
                rows = min(128, w_ap.shape[0] - 128 * ti)
                t_l = P.dma("sync", wtile[0:rows, 0:cols], w_ap[128 * ti:128 * ti + rows, :],
                            wait=[P.tok("eS"), P.tok("eV"), P.tok("eT")])
                wld = [P.tok("dS"), t_s2]
                t_q1 = P.emit("vector", lambda e, rows=rows, cols=cols, mi=mi: e.tensor_scalar(
                    wtile[0:rows, 0:cols], wtile[0:rows, 0:cols], scl[0:rows, 4 + mi:5 + mi], None,
                    op0=ALU.mult), wait=wld, inc="eV")
                t_q2 = P.emit("vector", lambda e, rows=rows, cols=cols: e.tensor_scalar(
                    qtile[0:rows, 0:cols], wtile[0:rows, 0:cols], 0.5, None, op0=ALU.is_gt),
                    wait=[t_q1], inc="eV")
                t_q3 = P.emit("vector", lambda e, rows=rows, cols=cols: e.tensor_scalar(
                    qtile2[0:rows, 0:cols], wtile[0:rows, 0:cols], -0.5, None, op0=ALU.is_lt),
                    wait=[t_q1], inc="eV")
                t_q5 = P.emit("vector", lambda e, rows=rows, cols=cols: e.tensor_sub(
                    qtile[0:rows, 0:cols], qtile[0:rows, 0:cols], qtile2[0:rows, 0:cols]),
                    wait=[t_q2, t_q3], inc="eV")
                t_q6 = P.emit("vector", lambda e, rows=rows, cols=cols, mi=mi: e.tensor_scalar(
                    qtile[0:rows, 0:cols], qtile[0:rows, 0:cols], scl[0:rows, mi:mi + 1], None,
                    op0=ALU.mult), wait=[t_q5], inc="eV")
                if nm != "wd":
                    for dch in range(4):
                        ps = ps_t if dch % 2 == 0 else ps_t2
                        t_tr = P.emit("tensor", lambda e, dch=dch, rows=rows, ps=ps: e.transpose(
                            ps[0:128, 0:rows], qtile[0:rows, 128 * dch:128 * (dch + 1)],
                            ident[0:rows, 0:rows]),
                            wait=[t_q6, t_ident, P.tok("dG")], inc="eT")
                        t_ev = P.emit("vector", lambda e, rows=rows, ps=ps: e.tensor_copy(
                            ttile[0:128, 0:rows], ps[0:128, 0:rows]),
                            wait=[t_tr, P.tok("dG")], inc="eV")
                        ocw = rows
                        base = off + _g_img_off(ti)
                        dst = wq_mine.ap().rearrange("a b -> (a b)")
                        dst2 = dst[base:base + 128 * 4 * ocw].rearrange(
                            "(p c) -> p c", c=4 * ocw)[0:128, dch * ocw:(dch + 1) * ocw]
                        P.dma("gpsimd", dst2, ttile[0:128, 0:rows], wait=[t_ev])
                else:
                    for oc in range(OC_N):
                        ocw = OCW[oc]
                        ps = ps_t if oc % 2 == 0 else ps_t2
                        t_tr = P.emit("tensor", lambda e, oc=oc, ocw=ocw, rows=rows, ps=ps:
                                      e.transpose(ps[0:ocw, 0:rows],
                                                  qtile[0:rows, OC_BASE[oc]:OC_BASE[oc] + ocw],
                                                  ident[:, :]),
                                      wait=[t_q6, t_ident, P.tok("dG")], inc="eT")
                        t_ev = P.emit("vector", lambda e, ocw=ocw, rows=rows, ps=ps: e.tensor_copy(
                            ttile[0:ocw, 0:rows], ps[0:ocw, 0:rows]),
                            wait=[t_tr, P.tok("dG")], inc="eV")
                        dst = wq_mine.ap().rearrange("a b -> (a b)")
                        region = dst[2 * GIMG + OC_BASE[oc] * 512:
                                     2 * GIMG + (OC_BASE[oc] + ocw) * 512].rearrange(
                            "(r c) -> r c", c=512)
                        dst3 = region[0:ocw, 128 * ti:128 * ti + rows]
                        P.dma("gpsimd", dst3, ttile[0:ocw, 0:rows], wait=[t_ev])
        t_ag = P.emit("gpsimd", lambda e: e.collective_compute(
            "AllGather", ALU.bypass, replica_groups=[list(range(8))],
            ins=[wq_mine.ap()], outs=[wq_all.ap()]),
            wait=[P.tok("dG"), P.tok("dS")], inc="cc1")
        P.barrier()
        P.flush("s4_quant")

    with ExitStack() as mst:
        msb = lambda name, shape, dt=F32: mst.enter_context(nc.sbuf_tensor(name, shape, dt))
        ps_g = mst.enter_context(nc.psum_tensor("s4_psg", [128, 512], F32))
        ps_u = mst.enter_context(nc.psum_tensor("s4_psu", [128, 512], F32))
        ps_d = [mst.enter_context(nc.psum_tensor(f"s4_psd{i}", [128, 512], F32))
                for i in range(6)]
        ps_down = [ps_g, ps_u] + ps_d

        ln4T = msb("s4_ln4T", [128, 32 * 512], BF16)
        hT = msb("s4_hT", [128, 88 * 512], BF16)
        hTn = msb("s4_hTn", [128, 88 * 128], BF16)
        gslc = msb("s4_gslc", [128, 2 * 4096], BF16)
        dslc = msb("s4_dslc", [128, 8 * 512], BF16)
        gsil = msb("s4_gsil", [128, 8 * 512], BF16)
        out_t = msb("s4_out", [128, D])
        z_tile = out_t
        tmp_bf = gslc[:, 0:D]
        colbuf = msb("s4_col", [128, 8])
        x_tile = msb("s4_xt", [128, D])

        for g in range(2):
            for tt in range(4):
                gt = 4 * g + tt
                t_x = P.dma("sync", out_t[:, :], x_in[128 * gt:128 * (gt + 1), :],
                            wait=[P.tok("eV"), P.tok("eS"), P.tok("eT"), P.tok("dG")])
                aux = {"colbuf": colbuf,
                       "x_ready": [P.tok("dS"), P.tok("eT"), P.tok("eV")]}
                emit_layernorm(P, out_t[:, :], z_tile[:, :], D, tmp_bf[:, :], aux)
                evac_prev = [P.tok("eV"), P.tok("eV")]
                for dch in range(32):
                    ps = ps_g if dch % 2 == 0 else ps_u
                    t_tr = P.emit("tensor", lambda e, dch=dch, ps=ps: e.transpose(
                        ps[:, 0:128], z_tile[:, 128 * dch:128 * (dch + 1)], identf[:, :]),
                        wait=[aux["t_out"], evac_prev[dch % 2]], inc="eT")
                    t_ev = P.emit("vector", lambda e, dch=dch, tt=tt, ps=ps: e.tensor_copy(
                        ln4T[:, 512 * dch + 128 * tt:512 * dch + 128 * (tt + 1)], ps[:, 0:128]),
                        wait=[t_tr], inc="eV")
                    evac_prev[dch % 2] = t_ev
            t_ln4T = P.tok("eV")

            for oc in range(OC_N):
                ocw = OCW[oc]
                w = 4 * ocw

                def load_slices(mat_off):
                    toks_w = [P.tok("eT"), P.tok("eS"), ("cc1", 1)]
                    for j in range(8):
                        src_off = mat_off + _g_img_off(oc)
                        src = wq_all.ap()[j:j + 1, src_off:src_off + 128 * w].rearrange(
                            "a (p c) -> (a p) c", p=128)
                        P.dma("sync", gslc[:, j * w:(j + 1) * w], src, wait=toks_w)
                    t_w = P.tok("dS")
                    return P.emit("vector", lambda e, w=w: e.tensor_scalar_mul(
                        gslc[:, 4096:4096 + 8 * w], gslc[:, 0:8 * w], -1.0),
                        wait=[t_w, P.tok("eT")], inc="eV")

                def mm_gu(e, k, ps, ocw=ocw, w=w):
                    r = None
                    for i in range(8):
                        j = int(J_TAB[i, k])
                        neg = 4096 if S_TAB[i, k] < 0 else 0
                        for dch in range(4):
                            r = e.matmul(
                                ps[0:ocw, :],
                                gslc[:, neg + j * w + dch * ocw: neg + j * w + (dch + 1) * ocw],
                                ln4T[:, 512 * (4 * i + dch):512 * (4 * i + dch + 1)],
                                start=(i == 0 and dch == 0), stop=(i == 7 and dch == 3))
                    return r

                t_ng = load_slices(0)
                t_sil = []
                for k in range(8):
                    ps = ps_g if k % 2 == 0 else ps_u
                    t_gmm = P.emit("tensor", lambda e, k=k, ps=ps, f=mm_gu: f(e, k, ps),
                                   wait=[t_ng, t_ln4T, P.tok("eS"), P.tok("eV")], inc="eT")
                    t_sil.append(P.emit("scalar", lambda e, k=k, ocw=ocw, ps=ps: e.activation(
                        out=gsil[0:ocw, 512 * k:512 * (k + 1)], in_=ps[0:ocw, :], func=AF.Silu),
                        wait=[t_gmm], inc="eS"))
                t_nu = load_slices(GIMG)
                for k in range(8):
                    hcol = (k * OC_N + oc) * 512
                    ps = ps_g if k % 2 == 0 else ps_u
                    t_umm = P.emit("tensor", lambda e, k=k, ps=ps, f=mm_gu: f(e, k, ps),
                                   wait=[t_nu, t_ln4T, P.tok("eV"), P.tok("eS")], inc="eT")
                    t_h = P.emit("vector", lambda e, k=k, ocw=ocw, hcol=hcol, ps=ps:
                                 e.tensor_tensor(hT[0:ocw, hcol:hcol + 512],
                                                 gsil[0:ocw, 512 * k:512 * (k + 1)],
                                                 ps[0:ocw, :], ALU.mult),
                                 wait=[t_umm, t_sil[k]], inc="eV")
            t_hT = P.tok("eV")
            if "dbg_hT" in io and g == 0:
                P.dma("sync", io["dbg_hT"][:, :], hT[:, 0:512], wait=[t_hT])
                P.dma("sync", io["dbg_ln4T"][:, :], ln4T[:, :], wait=[t_hT])
                P.dma("sync", io["dbg_gsil"][:, :], gsil[:, 0:512], wait=[t_hT])

            for tt in range(4):
                gt = 4 * g + tt
                hsrc = hT[:, :].rearrange("p (c t) -> p c t", c=88)[:, :, 128 * tt:128 * (tt + 1)]
                t_hn = P.emit("vector", lambda e, hsrc=hsrc: e.tensor_scalar_mul(
                    hTn[:, :].rearrange("p (c t) -> p c t", c=88), hsrc, -1.0),
                    wait=[t_hT, P.tok("eT")], inc="eV")
                t_ev_prev = [P.tok("eV")]
                mm_done = None
                for oc in range(OC_N):
                    ocw = OCW[oc]
                    for j in range(8):
                        src_off = 2 * GIMG + OC_BASE[oc] * 512
                        src = wq_all.ap()[j:j + 1, src_off:src_off + ocw * 512].rearrange(
                            "a (r c) -> (a r) c", c=512)
                        P.dma("sync", dslc[0:ocw, 512 * j:512 * (j + 1)], src,
                              wait=[P.tok("eT"), ("cc1", 1)])
                    t_wd = P.tok("dS")
                    def mm_down(e, oc=oc, ocw=ocw, tt=tt):
                        r = None
                        for k in range(8):
                            for k2 in range(8):
                                j = int(J_TAB[k, k2])
                                neg = S_TAB[k, k2] < 0
                                lh = (hTn if neg else hT)
                                if neg:
                                    lcol = (k * OC_N + oc) * 128
                                    lhs = hTn[0:ocw, lcol:lcol + 128]
                                else:
                                    lcol = (k * OC_N + oc) * 512 + 128 * tt
                                    lhs = hT[0:ocw, lcol:lcol + 128]
                                r = e.matmul(ps_down[k2][:, :], lhs,
                                             dslc[0:ocw, 512 * j:512 * (j + 1)],
                                             start=(oc == 0 and k == 0),
                                             stop=(oc == OC_N - 1 and k == 7))
                        return r
                    mm_done = P.emit("tensor", mm_down,
                                     wait=[t_hn, t_wd] + t_ev_prev, inc="eT")
                    t_ev_prev = []
                t_ev = []
                for k2 in range(8):
                    t_ev.append(P.emit("vector", lambda e, k2=k2: e.tensor_copy(
                        out_t[:, 512 * k2:512 * (k2 + 1)], ps_down[k2][:, :]),
                        wait=[mm_done], inc="eV"))
                t_x2 = P.dma("sync", x_tile[:, :], x_in[128 * gt:128 * (gt + 1), :],
                             wait=[P.tok("eV")])
                aux2 = {"colbuf": colbuf,
                        "x_ready": t_ev + [P.tok("eT"), P.tok("dG"), P.tok("dS")]}
                emit_layernorm(P, out_t[:, :], out_t[:, :], D, tmp_bf[:, :], aux2)
                t_add = P.emit("vector", lambda e: e.tensor_add(out_t[:, :], out_t[:, :],
                                                                x_tile[:, :]),
                               wait=[aux2["t_out"], P.tok("dS")], inc="eV")
                P.dma("sync", x_out[128 * gt:128 * (gt + 1), :], out_t[:, :], wait=[t_add])
        P.emit("sync", lambda e: e.nop(), wait=[P.tok("dS")])
        P.flush("s4_main")


def build_nc_stage4():
    nc = bass.Bass("TRN2", target_bir_lowering=False)
    io = {}
    io["x_in"] = nc.dram_tensor("x_in", [TPC, D], F32, kind="ExternalInput").ap()
    io["wg"] = nc.dram_tensor("wg", [1366, 512], F32, kind="ExternalInput").ap()
    io["wu"] = nc.dram_tensor("wu", [1366, 512], F32, kind="ExternalInput").ap()
    io["wd"] = nc.dram_tensor("wd", [512, 1366], F32, kind="ExternalInput").ap()
    io["x_out"] = nc.dram_tensor("x_out", [TPC, D], F32, kind="ExternalOutput").ap()
    io["ident"] = nc.inline_tensor(IDENT_BF, "identc").ap()
    io["identf"] = nc.inline_tensor(IDENT_F32, "identf").ap()
    io["wq_mine"] = nc.dram_tensor("wq_mine", [1, WQ_FLAT], BF16, kind="Internal")
    io["wq_all"] = nc.dram_tensor("wq_all", [8, WQ_FLAT], BF16, kind="Internal", addr_space="Shared")
    io["ar_in"] = nc.dram_tensor("ar_in", [4, 1], F32, kind="Internal")
    io["ar_out"] = nc.dram_tensor("ar_out", [4, 1], F32, kind="Internal")
    with ExitStack() as stack:
        P = Prog(nc, stack)
        build_stage4(P, io)
        P.flush("s4")
    return nc


def run_stage4(x3_full, w_gate, w_up, w_down):
    nc = build_nc_stage4()
    shards = shard_tokens(np.asarray(x3_full, dtype=np.float32))
    wg = np.ascontiguousarray(np.asarray(w_gate, dtype=np.float32))
    wu = np.ascontiguousarray(np.asarray(w_up, dtype=np.float32))
    wd = np.ascontiguousarray(np.asarray(w_down, dtype=np.float32))
    in_maps = [{
        "x_in": shards[i], "wg": wg[i], "wu": wu[i], "wd": wd[i],
    } for i in range(NCORES)]
    res = run_bass_kernel_spmd(nc, in_maps, core_ids=list(range(NCORES)))
    return unshard_tokens([r["x_out"] for r in res.results])


def _stage_scoped(P, builder, io):
    outer = P.stack
    with ExitStack() as st:
        P.stack = st
        builder(P, io)
    P.stack = outer
    P.barrier()


def build_nc_full():
    nc = bass.Bass("TRN2", target_bir_lowering=False)
    fre_i, fim_i, cinv_i = _s3_consts()
    identf = nc.inline_tensor(IDENT_F32, "identf").ap()
    identb = nc.inline_tensor(IDENT_BF, "identb").ap()
    ein = lambda n, shp, dt=F32: nc.dram_tensor(n, shp, dt, kind="ExternalInput").ap()
    x_in = ein("x_in", [TPC, D])
    x1 = nc.dram_tensor("x1buf", [TPC, D], F32, kind="Internal").ap()
    x2 = nc.dram_tensor("x2buf", [TPC, D], F32, kind="Internal").ap()
    x3 = nc.dram_tensor("x3buf", [TPC, D], F32, kind="Internal").ap()
    x_out = nc.dram_tensor("x_out", [TPC, D], F32, kind="ExternalOutput").ap()

    io1 = {"x_in": x_in, "x_out": x1, "identf": identf, "identb": identb,
           "wq_img": ein("wq_img", [48 * 128, 4096]),
           "wo_img": ein("wo_img", [8 * 128, 16384]),
           "ropecs": ein("ropecs", [128, 512]), "ropesn": ein("ropesn", [128, 512]),
           "snd": nc.dram_tensor("snd", [6144, 1024], BF16, kind="Internal"),
           "rcv": nc.dram_tensor("rcv", [6144, 1024], BF16, kind="Internal"),
           "snd2": nc.dram_tensor("snd2", [4096, 1024], BF16, kind="Internal"),
           "rcv2": nc.dram_tensor("rcv2", [4096, 1024], BF16, kind="Internal")}
    io2 = {"x_in": x1, "x_out": x2, "ident": identb,
           "w_cat": ein("w_cat", [D, 24]), "bases": ein("bases", [1, 24]),
           "gate": ein("gate", [1, 1]), "w_branch": ein("w_branch", [1024, 1024]),
           "eye_row": ein("eye_row", [1, 16])}
    io3 = {"x_in": x2, "x_out": x3, "identf": identf,
           "fre": ein("fre", [FCH * 128, 4096], BF16), "fim": ein("fim", [FCH * 128, 4096], BF16),
           "cinv": ein("cinv", [FCH * 128, NLAG], BF16),
           "dbl": nc.dram_tensor("dbl", [TPC, 2 * D], F32, kind="Internal")}
    io4 = {"x_in": x3, "x_out": x_out, "ident": identb, "identf": identf,
           "wg": ein("wg", [1366, 512]), "wu": ein("wu", [1366, 512]),
           "wd": ein("wd", [512, 1366]),
           "wq_mine": nc.dram_tensor("wq_mine", [1, WQ_FLAT], BF16, kind="Internal"),
           "wq_all": nc.dram_tensor("wq_all", [8, WQ_FLAT], BF16, kind="Internal",
                                    addr_space="Shared"),
           "ar_in": nc.dram_tensor("ar_in", [4, 1], F32, kind="Internal"),
           "ar_out": nc.dram_tensor("ar_out", [4, 1], F32, kind="Internal")}
    with ExitStack() as stack:
        P = Prog(nc, stack)
        _stage_scoped(P, build_stage1, io1)
        _stage_scoped(P, build_stage2, io2)
        _stage_scoped(P, build_stage3, io3)
        _stage_scoped(P, build_stage4, io4)
        P.flush("fin")
    return nc, (fre_i, fim_i, cinv_i)


def kernel(**inputs):
    nc, (fre_i, fim_i, cinv_i) = build_nc_full()
    shards = shard_tokens(np.asarray(inputs["L0_attn_x"], dtype=np.float32))
    wq_img = _wq_images(inputs["L0_attn_w_qkv"]).reshape(48 * 128, 4096)
    wo_img = _wo_images(inputs["L0_attn_w_o"]).reshape(8 * 128, 16384)
    wcat = _np_wcat(np.asarray(inputs["L0_ffn_W_res"]), np.asarray(inputs["L0_ffn_W_pre"]),
                    np.asarray(inputs["L0_ffn_W_post"]))
    bases = _np_bases(inputs["L0_ffn_H_res_base"], inputs["L0_ffn_H_pre_base"],
                      inputs["L0_ffn_H_post_base"])
    gate = np.asarray(inputs["L0_ffn_gate"], dtype=np.float32).reshape(1, 1)
    wbr = np.ascontiguousarray(np.asarray(inputs["L0_ffn_W_branch"], dtype=np.float32))
    wg = np.ascontiguousarray(np.asarray(inputs["L1_ffn_w_gate"], dtype=np.float32))
    wu = np.ascontiguousarray(np.asarray(inputs["L1_ffn_w_up"], dtype=np.float32))
    wd = np.ascontiguousarray(np.asarray(inputs["L1_ffn_w_down"], dtype=np.float32))
    in_maps = []
    for i in range(NCORES):
        cs, sn = _rope_tables(i)
        in_maps.append({
            "x_in": shards[i], "wq_img": wq_img, "wo_img": wo_img,
            "ropecs": cs, "ropesn": sn,
            "w_cat": wcat, "bases": bases, "gate": gate, "w_branch": wbr,
            "eye_row": EYE4_ROW,
            "fre": fre_i, "fim": fim_i, "cinv": cinv_i,
            "wg": wg[i], "wu": wu[i], "wd": wd[i],
        })
    res = run_bass_kernel_spmd(nc, in_maps, core_ids=list(range(NCORES)))
    out = unshard_tokens([r["x_out"] for r in res.results])
    return np.ascontiguousarray(out.astype(np.float32))


def kernel_unfused(**inputs):
    x1 = run_stage1(inputs["L0_attn_x"], inputs["L0_attn_w_qkv"], inputs["L0_attn_w_o"])
    x2 = run_stage2(x1, inputs["L0_ffn_W_res"], inputs["L0_ffn_W_pre"], inputs["L0_ffn_W_post"],
                    inputs["L0_ffn_H_res_base"], inputs["L0_ffn_H_pre_base"],
                    inputs["L0_ffn_H_post_base"], inputs["L0_ffn_gate"], inputs["L0_ffn_W_branch"])
    x3 = run_stage3(x2)
    out = run_stage4(x3, inputs["L1_ffn_w_gate"], inputs["L1_ffn_w_up"], inputs["L1_ffn_w_down"])
    return np.ascontiguousarray(out.astype(np.float32))


# version 03 (final)
# speedup vs baseline: 1.3438x; 1.3438x over previous
import math
from contextlib import ExitStack

import numpy as np
import ml_dtypes

import concourse.bass as bass
import concourse.mybir as mybir
from concourse.bass_utils import run_bass_kernel_spmd
from concourse.bass_core_rust import IndirectOffsetOnAxis

F32 = mybir.dt.float32
BF16 = mybir.dt.bfloat16
U32 = mybir.dt.uint32
AF = mybir.ActivationFunctionType
ALU = mybir.AluOpType
AX = mybir.AxisListType

NCORES = 8
B = 2
T = 4096
D = 4096
TPC = 1024
NT = TPC // 128
LN_EPS = 1e-5
RMS_EPS = 1.1920929e-07

bf16 = ml_dtypes.bfloat16



def shard_tokens(x):
    out = []
    for i in range(NCORES):
        out.append(np.ascontiguousarray(
            np.concatenate([x[0, 512 * i:512 * (i + 1)], x[1, 512 * i:512 * (i + 1)]], 0)))
    return out


def unshard_tokens(parts):
    x = np.empty((B, T, D), dtype=parts[0].dtype)
    for i in range(NCORES):
        x[0, 512 * i:512 * (i + 1)] = parts[i][:512]
        x[1, 512 * i:512 * (i + 1)] = parts[i][512:]
    return x



ENGS = ("sync", "vector", "scalar", "tensor", "gpsimd")


class Prog:
    def __init__(self, nc, stack):
        self.nc = nc
        self.stack = stack
        self.q = {e: [] for e in ENGS}
        self.sems = {}
        self.cnt = {}
        self.nblock = 0
        for name in ("dS", "dG", "dA", "eV", "eS", "eT", "eG", "eY",
                     "cc0", "cc1", "cc2", "cc3"):
            self.sems[name] = stack.enter_context(nc.semaphore(name))
            self.cnt[name] = 0
        self.junk = stack.enter_context(nc.sbuf_tensor("junk_bar", [128, 8], F32))
        self.kbuf = stack.enter_context(nc.sbuf_tensor("konst", [128, 16], F32))
        self.kmap = {}
        self.ktok = None

    def konst(self, value):
        v = float(value)
        if v not in self.kmap:
            idx = len(self.kmap)
            assert idx < 16
            self.kmap[v] = idx
            ap = self.kbuf[:, idx:idx + 1]
            self.ktok = self.emit("vector", lambda e, ap=ap, v=v: e.memset(ap, v), inc="eV")
        return self.kbuf[:, self.kmap[v]:self.kmap[v] + 1]

    def tok(self, sem):
        return (sem, self.cnt[sem])

    def emit(self, eng, fn, wait=(), inc=None, inc_by=1):
        waits = [w for w in wait if w is not None and w[1] > 0]
        if inc is not None:
            self.cnt[inc] += inc_by
        sems = self.sems

        def run(e, waits=tuple(waits), fn=fn, inc=inc, inc_by=inc_by):
            for (sname, v) in waits:
                e.wait_ge(sems[sname], v)
            r = fn(e)
            if inc is not None:
                r.then_inc(sems[inc], inc_by)
        self.q[eng].append(run)
        return (inc, self.cnt[inc]) if inc else None

    def dma(self, eng, out, in_, wait=(), sem=None, **kw):
        if sem is None:
            sem = {"sync": "dS", "gpsimd": "dG", "scalar": "dA"}[eng]
        return self.emit(eng, lambda e: e.dma_start(out=out, in_=in_, **kw),
                         wait=wait, inc=sem, inc_by=16)

    def barrier(self):
        j = self.junk
        tv = self.emit("vector", lambda e: e.tensor_copy(j[:, 0:1], j[:, 1:2]), inc="eV")
        ts = self.emit("scalar", lambda e: e.copy(j[:, 2:3], j[:, 3:4]), wait=[tv], inc="eS")
        tg = self.emit("gpsimd", lambda e: e.tensor_copy(j[:, 4:5], j[:, 5:6]), wait=[ts], inc="eG")
        tt = self.emit("tensor", lambda e: e.nop(), wait=[tg], inc="eT")
        ty = self.emit("sync", lambda e: e.nop(), wait=[tt], inc="eY")
        for eng in ("vector", "scalar", "gpsimd", "tensor"):
            self.emit(eng, lambda e: e.nop(), wait=[ty])
        return [ty]

    def flush(self, name=None):
        self.nblock += 1
        with self.nc.Block(name or f"blk{self.nblock}") as block:
            for eng in ENGS:
                fns = self.q[eng]
                if not fns:
                    continue
                getattr(block, eng)(lambda e, fns=tuple(fns): [f(e) for f in fns])
        self.q = {e: [] for e in ENGS}



def emit_layernorm(P, x_ap, out_ap, cols, tmp_bf, aux=None):
    nc = P.nc
    st = P.stack
    c = aux["colbuf"]
    wx = aux["x_ready"]
    t_sum = P.emit("vector", lambda e: e.tensor_reduce(c[:, 0:1], x_ap, axis=AX.X, op=ALU.add),
                   wait=wx, inc="eV")
    t_sq = P.emit("scalar", lambda e: e.activation(out=tmp_bf, in_=x_ap, func=AF.Square,
                                                   accum_out=c[:, 2:3]),
                  wait=wx, inc="eS")
    t_mean = P.emit("scalar", lambda e: e.mul(c[:, 1:2], c[:, 0:1], 1.0 / cols),
                    wait=[t_sum], inc="eS")
    t_m2 = P.emit("scalar", lambda e: e.activation(out=c[:, 3:4], in_=c[:, 1:2], func=AF.Square),
                  wait=[t_mean], inc="eS")
    t_var = P.emit("vector", lambda e: e.scalar_tensor_tensor(
        c[:, 4:5], c[:, 2:3], 1.0 / cols, c[:, 3:4], op0=ALU.mult, op1=ALU.subtract),
        wait=[t_sq, t_m2], inc="eV")
    eps_ap = P.konst(LN_EPS)
    t_std = P.emit("scalar", lambda e: e.activation(out=c[:, 5:6], in_=c[:, 4:5],
                                                    func=AF.Sqrt, bias=eps_ap),
                   wait=[t_var, P.ktok], inc="eS")
    t_rstd = P.emit("vector", lambda e: e.reciprocal(c[:, 6:7], c[:, 5:6]),
                    wait=[t_std], inc="eV")
    t_out = P.emit("vector", lambda e: e.tensor_scalar(
        out_ap, x_ap, c[:, 1:2], c[:, 6:7], op0=ALU.subtract, op1=ALU.mult),
        wait=[t_rstd], inc="eV")
    aux["t_var"] = t_var
    aux["t_out"] = t_out
    return t_out


def emit_transposes(P, src_bf_ap_fn, dst_sb, nchunks, ident, ps_pool, wait_src):
    evac_toks = [P.tok("eV"), P.tok("eV")]
    last = None
    for k in range(nchunks):
        ps = ps_pool[k % 2]
        w = list(wait_src) + [evac_toks[k % 2]]
        t_tr = P.emit("tensor", lambda e, k=k, ps=ps: e.transpose(ps[:, :], src_bf_ap_fn(k), ident[:, :]),
                      wait=w, inc="eT")
        last = P.emit("vector", lambda e, k=k, ps=ps: e.tensor_copy(dst_sb[:, 128 * k:128 * (k + 1)], ps[:, :]),
                      wait=[t_tr], inc="eV")
        evac_toks[k % 2] = last
    return last



def build_stage2(P, io):
    nc = P.nc
    st = P.stack
    x_in, x_out = io["x_in"], io["x_out"]

    sb = lambda name, shape, dt=F32: st.enter_context(nc.sbuf_tensor(name, shape, dt))
    psum = lambda name, shape, dt=F32: st.enter_context(nc.psum_tensor(name, shape, dt))

    ident = sb("s2_ident", [128, 128], BF16)
    w_cat = sb("s2_wcat", [128, 32 * 24], BF16)
    w_branch = sb("s2_wbr", [128, 8 * 1024], BF16)
    base_b = sb("s2_base", [128, 24])
    eye_b = sb("s2_eye", [128, 16])
    gate_b = sb("s2_gate", [128, 2])
    z_bf = [sb(f"s2_zbf{t}", [128, D], BF16) for t in range(NT)]
    zT = sb("s2_zT", [128, D], BF16)
    tmp_bf = sb("s2_tmp", [128, D], BF16)
    colbuf = [sb(f"s2_col{t}", [128, 8]) for t in range(NT)]
    ZA = sb("s2_ZA", [128, NT * 24])
    Zs = sb("s2_Zs", [128, NT * 16])
    u_sb = sb("s2_u", [128, NT * 4])
    v_sb = sb("s2_v", [128, NT * 4])
    m_sb = sb("s2_m", [128, NT * 4])
    e_sb = sb("s2_e", [128, NT * 16])
    s_sb = sb("s2_s", [128, NT * 4])
    ls_sb = sb("s2_ls", [128, NT * 4])
    H = sb("s2_H", [128, NT * 16])
    Hpre = sb("s2_Hpre", [128, NT * 4])
    Hpost = sb("s2_Hpost", [128, NT * 4])
    sm_sc = sb("s2_smsc", [128, 4])
    x_tile = sb("s2_x", [128, D])
    out_t = sb("s2_out", [128, D])
    xp = sb("s2_xp", [128, 1024])
    xp_bf = sb("s2_xpbf", [128, 1024], BF16)
    xpT = sb("s2_xpT", [128, 1024], BF16)
    y_sb = sb("s2_y", [128, 1024])
    ps_tr = [psum("s2_pstr0", [128, 128], BF16), psum("s2_pstr1", [128, 128], BF16)]
    ps_lg = psum("s2_pslg", [128, 24])
    ps_y = psum("s2_psy", [128, 512])

    t_ld_ident = P.dma("sync", ident[:, :], io["ident"][:, :])
    for k in range(32):
        P.dma("gpsimd", w_cat[:, 24 * k:24 * (k + 1)], io["w_cat"][128 * k:128 * (k + 1), :])
    for k in range(8):
        P.dma("gpsimd", w_branch[:, 1024 * k:1024 * (k + 1)], io["w_branch"][128 * k:128 * (k + 1), :])
    P.dma("sync", base_b[:, :], io["bases"][0:1, :].to_broadcast([128, 24]))
    P.dma("sync", eye_b[:, :], io["eye_row"][0:1, :].to_broadcast([128, 16]))
    P.dma("sync", gate_b[:, 0:1], io["gate"][0:1, :].to_broadcast([128, 1]))
    t_consts_s = P.tok("dS")
    t_consts_g = P.tok("dG")

    for t in range(NT):
        t_x = P.dma("sync", x_tile[:, :], x_in[128 * t:128 * (t + 1), :],
                    wait=[P.tok("eV")])
        aux = {"colbuf": colbuf[t], "x_ready": [P.tok("dS")]}
        emit_layernorm(P, x_tile[:, :], z_bf[t][:, :], D, tmp_bf[:, :], aux)
        t_z = aux["t_out"]
        c = colbuf[t]
        t_r1 = P.emit("vector", lambda e, c=c: e.tensor_mul(c[:, 7:8], c[:, 4:5], c[:, 6:7]),
                      wait=[aux["t_out"]], inc="eV")
        t_r2 = P.emit("vector", lambda e, c=c: e.tensor_mul(c[:, 7:8], c[:, 7:8], c[:, 6:7]),
                      wait=[t_r1], inc="eV")
        rms_eps_ap = P.konst(RMS_EPS)
        t_r3 = P.emit("scalar", lambda e, c=c: e.activation(out=c[:, 7:8], in_=c[:, 7:8],
                                                            func=AF.Sqrt, bias=rms_eps_ap),
                      wait=[t_r2, P.ktok], inc="eS")
        t_rms = P.emit("vector", lambda e, c=c: e.reciprocal(c[:, 7:8], c[:, 7:8]),
                       wait=[t_r3], inc="eV")
        t_tr = emit_transposes(P, lambda k, t=t: z_bf[t][:, 128 * k:128 * (k + 1)],
                               zT, 32, ident, ps_tr,
                               wait_src=[t_z, t_ld_ident, P.tok("eT")])
        def mm_logits(e, t=t):
            for k in range(32):
                r = e.matmul(ps_lg[:, :], zT[:, 128 * k:128 * (k + 1)],
                             w_cat[:, 24 * k:24 * (k + 1)], start=(k == 0), stop=(k == 31))
            return r
        t_mm = P.emit("tensor", mm_logits, wait=[t_tr, t_consts_g, P.tok("eV")], inc="eT")
        t_ev = P.emit("vector", lambda e, t=t, c=c: e.tensor_scalar(
            ZA[:, 24 * t:24 * (t + 1)], ps_lg[:, :], c[:, 7:8], None, op0=ALU.mult),
            wait=[t_mm, t_rms], inc="eV")
        P.emit("vector", lambda e, t=t: e.tensor_add(
            ZA[:, 24 * t:24 * (t + 1)], ZA[:, 24 * t:24 * (t + 1)], base_b[:, :]),
            wait=[t_ev, t_consts_s], inc="eV")
    t_za = P.tok("eV")

    za4 = ZA[:, :].rearrange("p (t c) -> p t c", t=NT)
    t_zs = P.emit("vector", lambda e: e.tensor_scalar_mul(
        Zs[:, :].rearrange("p (t c) -> p t c", t=NT), za4[:, :, 0:16], 10.0),
        wait=[t_za], inc="eV")
    P.emit("vector", lambda e: e.memset(u_sb[:, :], 0.0), inc="eV")
    t_v0 = P.emit("vector", lambda e: e.memset(v_sb[:, :], 0.0), inc="eV")

    Z4 = Zs[:, :].rearrange("p (t i j) -> p t i j", t=NT, i=4, j=4)
    Z4T = Z4.transpose([0, 1, 3, 2])
    m4 = m_sb[:, :].rearrange("p (t i) -> p t i", t=NT)
    u4 = u_sb[:, :].rearrange("p (t i) -> p t i", t=NT)
    v4 = v_sb[:, :].rearrange("p (t j) -> p t j", t=NT)
    e4 = e_sb[:, :].rearrange("p (t i j) -> p t i j", t=NT, i=4, j=4)
    e4T = e4.transpose([0, 1, 3, 2])
    shp4 = [128, NT, 4, 4]

    def lse_step(P, Zv, ee, out_flat, bcast_col4, wait):
        vb = bcast_col4.unsqueeze(2).to_broadcast(shp4)
        t1 = P.emit("vector", lambda e: e.tensor_tensor(ee, Zv, vb, ALU.add),
                    wait=wait, inc="eV")
        t2 = P.emit("vector", lambda e: e.tensor_reduce(m4, ee, axis=AX.X, op=ALU.max),
                    wait=[t1], inc="eV")
        mb = m4.unsqueeze(3).to_broadcast(shp4)
        t3 = P.emit("vector", lambda e: e.tensor_tensor(ee, ee, mb, ALU.subtract),
                    wait=[t2], inc="eV")
        t4 = P.emit("scalar", lambda e: e.activation(out=e_sb[:, :], in_=e_sb[:, :], func=AF.Exp),
                    wait=[t3], inc="eS")
        t5 = P.emit("vector", lambda e: e.tensor_reduce(s_sb[:, :].rearrange("p (t i) -> p t i", t=NT),
                                                        ee, axis=AX.X, op=ALU.add),
                    wait=[t4], inc="eV")
        t6 = P.emit("scalar", lambda e: e.activation(out=ls_sb[:, :], in_=s_sb[:, :], func=AF.Ln),
                    wait=[t5], inc="eS")
        t7 = P.emit("vector", lambda e: e.tensor_tensor(ls_sb[:, :], ls_sb[:, :], m_sb[:, :], ALU.add),
                    wait=[t6], inc="eV")
        t8 = P.emit("vector", lambda e: e.tensor_scalar_mul(out_flat, ls_sb[:, :], -1.0),
                    wait=[t7], inc="eV")
        return t8

    t_prev = [t_zs, t_v0]
    for it in range(50):
        tu = lse_step(P, Z4, e4, u_sb[:, :], v4, wait=t_prev)
        tv = lse_step(P, Z4T, e4T, v_sb[:, :], u4, wait=[tu])
        t_prev = [tv]

    ub = u4.unsqueeze(3).to_broadcast(shp4)
    vb2 = v4.unsqueeze(2).to_broadcast(shp4)
    H4 = H[:, :].rearrange("p (t i j) -> p t i j", t=NT, i=4, j=4)
    t_h1 = P.emit("vector", lambda e: e.tensor_tensor(H4, Z4, ub, ALU.add), wait=t_prev, inc="eV")
    t_h2 = P.emit("vector", lambda e: e.tensor_tensor(H4, H4, vb2, ALU.add), wait=[t_h1], inc="eV")
    t_h3 = P.emit("scalar", lambda e: e.activation(out=H[:, :], in_=H[:, :], func=AF.Exp),
                  wait=[t_h2], inc="eS")
    t_g = P.emit("scalar", lambda e: e.activation(out=gate_b[:, 1:2], in_=gate_b[:, 0:1],
                                                  func=AF.Sigmoid),
                 wait=[t_consts_s], inc="eS")
    t_g1 = P.emit("vector", lambda e: e.tensor_scalar(sm_sc[:, 0:1], gate_b[:, 1:2],
                                                      -1.0, 1.0, op0=ALU.mult, op1=ALU.add),
                  wait=[t_g], inc="eV")
    t_h4 = P.emit("vector", lambda e: e.tensor_scalar(H[:, :], H[:, :], gate_b[:, 1:2], None,
                                                      op0=ALU.mult),
                  wait=[t_h3, t_g1], inc="eV")
    t_ei2 = P.emit("vector", lambda e: e.tensor_scalar(
        e_sb[:, 0:16], eye_b[:, :], sm_sc[:, 0:1], None, op0=ALU.mult),
        wait=[t_g1, t_prev[0]], inc="eV")
    Hv = H[:, :].rearrange("p (t c) -> p t c", t=NT)
    eyeb = e_sb[:, 0:16].unsqueeze(1).to_broadcast([128, NT, 16])
    t_H = P.emit("vector", lambda e: e.tensor_tensor(Hv, Hv, eyeb, ALU.add),
                 wait=[t_h4, t_ei2], inc="eV")

    def softmax4(P, src_view, dst, wait):
        d4 = dst[:, :].rearrange("p (t i) -> p t i", t=NT)
        mm = m_sb[:, :].rearrange("p (t i) -> p t i", t=NT)[:, :, 0:1]
        ss = s_sb[:, :].rearrange("p (t i) -> p t i", t=NT)[:, :, 0:1]
        rr = ls_sb[:, :].rearrange("p (t i) -> p t i", t=NT)[:, :, 0:1]
        t1 = P.emit("vector", lambda e: e.tensor_reduce(mm, src_view, axis=AX.X, op=ALU.max),
                    wait=wait, inc="eV")
        t2 = P.emit("vector", lambda e: e.tensor_tensor(
            d4, src_view, mm.to_broadcast([128, NT, 4]), ALU.subtract),
            wait=[t1], inc="eV")
        t3 = P.emit("scalar", lambda e: e.activation(out=dst[:, :], in_=dst[:, :], func=AF.Exp),
                    wait=[t2], inc="eS")
        t4 = P.emit("vector", lambda e: e.tensor_reduce(ss, d4, axis=AX.X, op=ALU.add),
                    wait=[t3], inc="eV")
        t5 = P.emit("vector", lambda e: e.reciprocal(rr, ss), wait=[t4], inc="eV")
        t6 = P.emit("vector", lambda e: e.tensor_tensor(
            d4, d4, rr.to_broadcast([128, NT, 4]), ALU.mult),
            wait=[t5], inc="eV")
        return t6

    t_hpre = softmax4(P, za4[:, :, 16:20], Hpre, [t_za, t_H])
    t_hpost = softmax4(P, za4[:, :, 20:24], Hpost, [t_hpre])

    Hpre4 = Hpre[:, :]
    Hpost4 = Hpost[:, :]
    for t in range(NT):
        zt = z_bf[t]
        t_xp = P.emit("vector", lambda e, t=t, zt=zt: e.tensor_scalar(
            xp[:, :], zt[:, 0:1024], Hpre4[:, 4 * t:4 * t + 1], None, op0=ALU.mult),
            wait=[t_hpre, P.tok("eS")], inc="eV")
        for n in range(1, 4):
            t_xp = P.emit("vector", lambda e, t=t, n=n, zt=zt: e.scalar_tensor_tensor(
                xp[:, :], zt[:, 1024 * n:1024 * (n + 1)], Hpre4[:, 4 * t + n:4 * t + n + 1],
                xp[:, :], op0=ALU.mult, op1=ALU.add),
                wait=[t_xp], inc="eV")
        t_xpbf = P.emit("scalar", lambda e: e.copy(xp_bf[:, :], xp[:, :]), wait=[t_xp], inc="eS")
        t_xpT = emit_transposes(P, lambda k: xp_bf[:, 128 * k:128 * (k + 1)],
                                xpT, 8, ident, ps_tr, wait_src=[t_xpbf, P.tok("eT")])
        t_ymm = []
        for cc in range(2):
            def mm_y(e, cc=cc):
                for k in range(8):
                    r = e.matmul(ps_y[:, :], xpT[:, 128 * k:128 * (k + 1)],
                                 w_branch[:, 1024 * k + 512 * cc:1024 * k + 512 * (cc + 1)],
                                 start=(k == 0), stop=(k == 7))
                return r
            tm = P.emit("tensor", mm_y, wait=[t_xpT, t_consts_g, P.tok("eV")], inc="eT")
            t_ymm.append(P.emit("vector", lambda e, cc=cc: e.tensor_copy(
                y_sb[:, 512 * cc:512 * (cc + 1)], ps_y[:, :]), wait=[tm], inc="eV"))
        t_acc = []
        for i in range(4):
            tt = P.emit("vector", lambda e, t=t, i=i, zt=zt: e.tensor_scalar(
                out_t[:, 1024 * i:1024 * (i + 1)], zt[:, 0:1024],
                H[:, 16 * t + 4 * i:16 * t + 4 * i + 1], None, op0=ALU.mult),
                wait=[t_H, P.tok("dS")], inc="eV")
            for jj in range(1, 4):
                tt = P.emit("vector", lambda e, t=t, i=i, jj=jj, zt=zt: e.scalar_tensor_tensor(
                    out_t[:, 1024 * i:1024 * (i + 1)], zt[:, 1024 * jj:1024 * (jj + 1)],
                    H[:, 16 * t + 4 * i + jj:16 * t + 4 * i + jj + 1],
                    out_t[:, 1024 * i:1024 * (i + 1)], op0=ALU.mult, op1=ALU.add),
                    wait=[tt], inc="eV")
            tt = P.emit("vector", lambda e, t=t, i=i: e.scalar_tensor_tensor(
                out_t[:, 1024 * i:1024 * (i + 1)], y_sb[:, :],
                Hpost4[:, 4 * t + i:4 * t + i + 1],
                out_t[:, 1024 * i:1024 * (i + 1)], op0=ALU.mult, op1=ALU.add),
                wait=[tt, t_hpost] + t_ymm, inc="eV")
            t_acc.append(tt)
        t_x = P.dma("sync", x_tile[:, :], x_in[128 * t:128 * (t + 1), :], wait=[P.tok("eV")])
        aux2 = {"colbuf": colbuf[t], "x_ready": t_acc}
        emit_layernorm(P, out_t[:, :], out_t[:, :], D, tmp_bf[:, :], aux2)
        t_add = P.emit("vector", lambda e, t=t: e.tensor_add(out_t[:, :], out_t[:, :], x_tile[:, :]),
                       wait=[aux2["t_out"], P.tok("dS")], inc="eV")
        P.dma("sync", x_out[128 * t:128 * (t + 1), :], out_t[:, :], wait=[t_add])
    P.emit("sync", lambda e: e.nop(), wait=[P.tok("dS")])
    P.flush("s2")


def _np_wcat(L0_ffn_W_res, L0_ffn_W_pre, L0_ffn_W_post):
    return np.ascontiguousarray(
        np.concatenate([L0_ffn_W_res.T, L0_ffn_W_pre.T, L0_ffn_W_post.T], axis=1))


def _np_bases(H_res_base, H_pre_base, H_post_base):
    return np.concatenate([np.asarray(H_res_base).reshape(-1),
                           np.asarray(H_pre_base).reshape(-1),
                           np.asarray(H_post_base).reshape(-1)])[None, :].astype(np.float32)


IDENT_BF = np.eye(128, dtype=bf16)
IDENT_F32 = np.eye(128, dtype=np.float32)
EYE4_ROW = np.eye(4, dtype=np.float32).reshape(1, 16)


def build_nc_stage2():
    nc = bass.Bass("TRN2", target_bir_lowering=False)
    io = {}
    io["x_in"] = nc.dram_tensor("x_in", [TPC, D], F32, kind="ExternalInput").ap()
    io["w_cat"] = nc.dram_tensor("w_cat", [D, 24], F32, kind="ExternalInput").ap()
    io["bases"] = nc.dram_tensor("bases", [1, 24], F32, kind="ExternalInput").ap()
    io["gate"] = nc.dram_tensor("gate", [1, 1], F32, kind="ExternalInput").ap()
    io["w_branch"] = nc.dram_tensor("w_branch", [1024, 1024], F32, kind="ExternalInput").ap()
    io["eye_row"] = nc.dram_tensor("eye_row", [1, 16], F32, kind="ExternalInput").ap()
    io["x_out"] = nc.dram_tensor("x_out", [TPC, D], F32, kind="ExternalOutput").ap()
    io["ident"] = nc.inline_tensor(IDENT_BF, "identc").ap()
    with ExitStack() as stack:
        P = Prog(nc, stack)
        build_stage2(P, io)
    return nc


def run_stage2(x1_full, L0_ffn_W_res, L0_ffn_W_pre, L0_ffn_W_post,
               L0_ffn_H_res_base, L0_ffn_H_pre_base, L0_ffn_H_post_base,
               L0_ffn_gate, L0_ffn_W_branch):
    nc = build_nc_stage2()
    shards = shard_tokens(np.asarray(x1_full, dtype=np.float32))
    wcat = _np_wcat(np.asarray(L0_ffn_W_res), np.asarray(L0_ffn_W_pre), np.asarray(L0_ffn_W_post))
    bases = _np_bases(L0_ffn_H_res_base, L0_ffn_H_pre_base, L0_ffn_H_post_base)
    gate = np.asarray(L0_ffn_gate, dtype=np.float32).reshape(1, 1)
    wbr = np.ascontiguousarray(np.asarray(L0_ffn_W_branch, dtype=np.float32))
    in_maps = [{
        "x_in": s, "w_cat": wcat, "bases": bases, "gate": gate,
        "w_branch": wbr, "eye_row": EYE4_ROW,
    } for s in shards]
    res = run_bass_kernel_spmd(nc, in_maps, core_ids=list(range(NCORES)))
    return unshard_tokens([r["x_out"] for r in res.results])



def _rope_tables(core_id):
    pos = 512 * core_id + np.arange(512, dtype=np.float64)
    inv = 1.0 / (10000.0 ** (np.arange(0, 128, 2, dtype=np.float64) / 128.0))
    ang = np.outer(inv, pos)
    cs = np.repeat(np.cos(ang), 2, axis=0)
    sn = np.repeat(np.sin(ang), 2, axis=0)
    sn[0::2, :] *= -1.0
    return cs.astype(np.float32), sn.astype(np.float32)


def _wq_images(w_qkv):
    w = np.asarray(w_qkv, dtype=np.float32).reshape(32, 128, 48, 128)
    return np.ascontiguousarray(w.transpose(2, 1, 0, 3).reshape(48, 128, 4096))


def _wo_images(w_o):
    w = np.asarray(w_o, dtype=np.float32).reshape(32, 128, 8, 512)
    return np.ascontiguousarray(w.transpose(2, 1, 0, 3).reshape(8, 128, 16384))


SWAP_MASK = [i ^ 1 for i in range(32)]


def build_stage1(P, io):
    nc = P.nc
    st = P.stack
    sb = lambda name, shape, dt=F32: st.enter_context(nc.sbuf_tensor(name, shape, dt))
    x_in, x_out = io["x_in"], io["x_out"]
    snd, rcv = io["snd"], io["rcv"]
    snd2, rcv2 = io["snd2"], io["rcv2"]
    from concourse.aot_env import allow_ici_all_to_all

    identf = sb("s1_identf", [128, 128], F32)
    identb = sb("s1_identb", [128, 128], BF16)
    t_idf = P.dma("sync", identf[:, :], io["identf"][:, :])
    t_idb = P.dma("sync", identb[:, :], io["identb"][:, :])
    colbuf = sb("s1_col", [128, 8])

    with nc.psum_tensor("s1_psa0", [128, 512], F32) as ps_a0, \
         nc.psum_tensor("s1_psa1", [128, 512], F32) as ps_a1, \
         nc.sbuf_tensor("s1_lnT", [128, 64 * 512], BF16) as lnT, \
         nc.sbuf_tensor("s1_wq0", [128, 4096], BF16) as wq0, \
         nc.sbuf_tensor("s1_wq1", [128, 4096], BF16) as wq1, \
         nc.sbuf_tensor("s1_cs", [128, 512], BF16) as cs_b, \
         nc.sbuf_tensor("s1_sn", [128, 512], BF16) as sn_b, \
         nc.sbuf_tensor("s1_ro", [128, 512], BF16) as ro_bf, \
         nc.sbuf_tensor("s1_sw", [128, 512], BF16) as sw_bf, \
         nc.sbuf_tensor("s1_t2", [128, 512], BF16) as t2_bf, \
         nc.sbuf_tensor("s1_x", [128, D], F32) as x_tile, \
         nc.sbuf_tensor("s1_z", [128, D], F32) as z_tile, \
         nc.sbuf_tensor("s1_tmp", [128, D], BF16) as tmp_bf:
        P.dma("gpsimd", cs_b[:, :], io["ropecs"][:, :])
        P.dma("gpsimd", sn_b[:, :], io["ropesn"][:, :])
        t_rope_c = P.tok("dG")
        ps_tr = [ps_a0, ps_a1]
        for g in range(2):
            for tt in range(4):
                gt = 4 * g + tt
                t_x = P.dma("sync", x_tile[:, :], x_in[128 * gt:128 * (gt + 1), :],
                            wait=[P.tok("eV"), P.tok("eS")])
                aux = {"colbuf": colbuf, "x_ready": [P.tok("dS"), P.tok("eV"), P.tok("eS")]}
                emit_layernorm(P, x_tile[:, :], z_tile[:, :], D, tmp_bf[:, :], aux)
                evac_prev = [P.tok("eV"), P.tok("eV")]
                for dch in range(32):
                    ps = ps_tr[dch % 2]
                    t_tr = P.emit("tensor", lambda e, dch=dch, ps=ps: e.transpose(
                        ps[:, 0:128], z_tile[:, 128 * dch:128 * (dch + 1)], identf[:, :]),
                        wait=[aux["t_out"], t_idf, evac_prev[dch % 2]], inc="eT")
                    t_ev = P.emit("vector", lambda e, g=g, dch=dch, tt=tt, ps=ps: e.tensor_copy(
                        lnT[:, 512 * (32 * g + dch) + 128 * tt:512 * (32 * g + dch) + 128 * (tt + 1)],
                        ps[:, 0:128]),
                        wait=[t_tr], inc="eV")
                    evac_prev[dch % 2] = t_ev
        t_lnT = P.tok("eV")
        wqb = [wq0, wq1]
        for cc in range(48):
            wq = wqb[cc % 2]
            t_w = P.dma("gpsimd", wq[:, :], io["wq_img"][cc].squeeze(0) if False else
                        io["wq_img"][128 * cc:128 * (cc + 1), :],
                        wait=[P.tok("eT")])
            if cc < 32:
                row = 768 * (cc // 4) + 128 * (cc % 4)
            elif cc < 40:
                row = 768 * (cc - 32) + 512
            else:
                row = 768 * (cc - 40) + 640
            for g in range(2):
                ps = ps_tr[g]
                def mm_qkv(e, cc=cc, g=g, ps=ps, wq=wq):
                    r = None
                    for dch in range(32):
                        r = e.matmul(ps[:, :], wq[:, 128 * dch:128 * (dch + 1)],
                                     lnT[:, 512 * (32 * g + dch):512 * (32 * g + dch + 1)],
                                     start=(dch == 0), stop=(dch == 31))
                    return r
                t_mm = P.emit("tensor", mm_qkv,
                              wait=[P.tok("dG"), t_lnT, P.tok("eS"), P.tok("eV")], inc="eT")
                t_cp = P.emit("scalar", lambda e, ps=ps: e.copy(ro_bf[:, :], ps[:, :]),
                              wait=[t_mm, P.tok("dG"), P.tok("eV")], inc="eS")
                if cc < 40:
                    t_sw = P.emit("vector", lambda e: e.stream_shuffle(sw_bf[:, :], ro_bf[:, :],
                                                                       SWAP_MASK),
                                  wait=[t_cp, t_rope_c], inc="eV")
                    t_r1 = P.emit("vector", lambda e: e.tensor_mul(ro_bf[:, :], ro_bf[:, :],
                                                                   cs_b[:, :]),
                                  wait=[t_sw], inc="eV")
                    t_r2 = P.emit("vector", lambda e: e.tensor_mul(t2_bf[:, :], sw_bf[:, :],
                                                                   sn_b[:, :]),
                                  wait=[t_sw], inc="eV")
                    t_ro = P.emit("vector", lambda e: e.tensor_add(ro_bf[:, :], ro_bf[:, :],
                                                                   t2_bf[:, :]),
                                  wait=[t_r1, t_r2], inc="eV")
                else:
                    t_ro = t_cp
                P.dma("gpsimd", snd.ap()[row:row + 128, 512 * g:512 * (g + 1)], ro_bf[:, :],
                      wait=[t_ro])
        t_a2a1 = P.emit("gpsimd", lambda e: _cc_a2a(e, snd, rcv),
                        wait=[P.tok("dG"), P.tok("dS")], inc="cc2")
    P.barrier()
    P.flush("s1_A")

    with ExitStack() as cst:
        en = lambda cm: cst.enter_context(cm)
        ps_s0 = en(nc.psum_tensor("s1_pss0", [128, 512], F32))
        ps_s1 = en(nc.psum_tensor("s1_pss1", [128, 512], F32))
        ps_o = en(nc.psum_tensor("s1_pso", [128, 512], F32))
        ps_d = en(nc.psum_tensor("s1_psd", [128, 512], F32))
        ps_b = en(nc.psum_tensor("s1_psb", [128, 512], F32))
        ps_tr2 = en(nc.psum_tensor("s1_ptr", [128, 128], BF16))
        kT = en(nc.sbuf_tensor("s1_kT", [128, 32 * 128], BF16))
        vT = en(nc.sbuf_tensor("s1_vT", [128, 32 * 128], BF16))
        V_all = en(nc.sbuf_tensor("s1_V", [128, 32 * 128], BF16))
        qT = en(nc.sbuf_tensor("s1_qT", [128, 32 * 128], BF16))
        pb0 = en(nc.sbuf_tensor("s1_pb0", [128, 512], BF16))
        pb1 = en(nc.sbuf_tensor("s1_pb1", [128, 512], BF16))
        attnT = en(nc.sbuf_tensor("s1_aT", [128, 512], BF16))
        dn_bf = en(nc.sbuf_tensor("s1_dnb", [128, 512], BF16))
        dnrow = en(nc.sbuf_tensor("s1_dn", [1, 1024], F32))
        dinv_bf = en(nc.sbuf_tensor("s1_dnr", [1, 512], BF16))
        ones_bf = en(nc.sbuf_tensor("s1_on", [128, 2], BF16))
        identb2 = en(nc.sbuf_tensor("s1_idb2", [128, 128], BF16))
        pbs = [pb0, pb1]
        pss = [ps_s0, ps_s1]
        t_idb2 = P.emit("vector", lambda e: e.tensor_copy(identb2[:, :], identb[:, :]),
                        wait=[t_idb], inc="eV")
        t_ones = P.emit("vector", lambda e: e.memset(ones_bf[:, :], 1.0), inc="eV")
        m20 = P.konst(-20.0)
        for b in range(2):
            for r in range(8):
                P.dma("sync", kT[:, 512 * r:512 * (r + 1)],
                      rcv.ap()[768 * r + 512:768 * r + 640, 512 * b:512 * (b + 1)],
                      wait=[("cc2", 1), P.tok("eT"), P.tok("eV")])
                P.dma("sync", vT[:, 512 * r:512 * (r + 1)],
                      rcv.ap()[768 * r + 640:768 * r + 768, 512 * b:512 * (b + 1)],
                      wait=[("cc2", 1)])
            t_kv = [P.tok("dS")]
            t_vta = P.tok("eV")
            for jt in range(32):
                t_vt = P.emit("tensor", lambda e, jt=jt: e.transpose(
                    ps_tr2[:, :], vT[:, 128 * jt:128 * (jt + 1)], identb2[:, :]),
                    wait=t_kv + [t_idb2, t_vta], inc="eT")
                t_vta = P.emit("vector", lambda e, jt=jt: e.tensor_copy(
                    V_all[:, 128 * jt:128 * (jt + 1)], ps_tr2[:, :]),
                    wait=[t_vt], inc="eV")
            t_vready = t_vta
            for h in range(4):
                for r in range(8):
                    P.dma("sync", qT[:, 512 * r:512 * (r + 1)],
                          rcv.ap()[768 * r + 128 * h:768 * r + 128 * (h + 1),
                                   512 * b:512 * (b + 1)],
                          wait=[("cc2", 1), P.tok("eT")])
                t_q = P.tok("dS")
                for qc in range(8):
                    njt = 4 * qc + 4
                    t_exp = {}
                    t_pe = {}
                    for jt in range(njt):
                        w = [t_q, t_vready]
                        if jt - 2 in t_exp:
                            w.append(t_exp[jt - 2])
                        elif jt < 2:
                            w.append(P.tok("eS"))
                        t_sc = P.emit("tensor", lambda e, jt=jt, qc=qc: e.matmul(
                            pss[jt % 2][:, :], kT[:, 128 * jt:128 * (jt + 1)],
                            qT[:, 512 * qc:512 * (qc + 1)], start=True, stop=True),
                            wait=w, inc="eT")
                        pj = jt - 1
                        if pj >= 0:
                            w2 = [t_exp[pj], t_ones] + ([P.tok("eV")] if pj == 0 else [])
                            def pv_group(e, pj=pj, qc=qc):
                                pb = pbs[pj % 2]
                                e.matmul(ps_o[:, :], V_all[:, 128 * pj:128 * (pj + 1)],
                                         pb[:, :], start=(pj == 0), stop=(pj == njt - 1))
                                return e.matmul(ps_d[0:1, :], ones_bf[:, 0:1], pb[:, :],
                                                start=(pj == 0), stop=(pj == njt - 1))
                            t_pe[pj] = P.emit("tensor", pv_group, wait=w2, inc="eT")
                        pb = pbs[jt % 2]
                        w_e = [t_sc, P.ktok]
                        if jt - 2 in t_pe:
                            w_e.append(t_pe[jt - 2])
                        t_ex = P.emit("scalar", lambda e, jt=jt, pb=pb: e.activation(
                            out=pb[:, :], in_=pss[jt % 2][:, :], func=AF.Exp,
                            scale=float(1.0 / math.sqrt(128.0)), bias=m20),
                            wait=w_e, inc="eS")
                        m = jt - 4 * qc
                        if m >= 0:
                            t_ex = P.emit("gpsimd", lambda e, m=m, pb=pb: e.affine_select(
                                pb[:, :], pb[:, :], pattern=[[1, 512]],
                                compare_op=ALU.is_ge, fill=0.0, base=-128 * m,
                                channel_multiplier=-1),
                                wait=[t_ex], inc="eG")
                        t_exp[jt] = t_ex
                    pj = njt - 1
                    w2 = [t_exp[pj], t_ones] + ([P.tok("eV")] if pj == 0 else [])
                    def pv_last(e, pj=pj, qc=qc, njt=njt):
                        pb = pbs[pj % 2]
                        e.matmul(ps_o[:, :], V_all[:, 128 * pj:128 * (pj + 1)],
                                 pb[:, :], start=(pj == 0), stop=(pj == njt - 1))
                        return e.matmul(ps_d[0:1, :], ones_bf[:, 0:1], pb[:, :],
                                        start=(pj == 0), stop=(pj == njt - 1))
                    t_pv = P.emit("tensor", pv_last, wait=w2, inc="eT")
                    t_d1 = P.emit("vector", lambda e: e.tensor_copy(dnrow[0:1, 0:512],
                                                                    ps_d[0:1, :]),
                                  wait=[t_pv], inc="eV")
                    t_d2 = P.emit("scalar", lambda e: e.activation(
                        out=dnrow[0:1, 512:1024], in_=dnrow[0:1, 0:512], func=AF.Ln),
                        wait=[t_d1], inc="eS")
                    t_d3 = P.emit("scalar", lambda e: e.activation(
                        out=dinv_bf[0:1, :], in_=dnrow[0:1, 512:1024], func=AF.Exp,
                        scale=-1.0),
                        wait=[t_d2], inc="eS")
                    t_bc = P.emit("tensor", lambda e: e.matmul(
                        ps_b[:, :], ones_bf[0:1, 0:1].to_broadcast([1, 128]),
                        dinv_bf[0:1, :], start=True, stop=True),
                        wait=[t_d3, P.tok("eV")], inc="eT")
                    t_db = P.emit("vector", lambda e: e.tensor_copy(dn_bf[:, :], ps_b[:, :]),
                                  wait=[t_bc], inc="eV")
                    t_at = P.emit("vector", lambda e: e.tensor_tensor(attnT[:, :], dn_bf[:, :],
                                                                      ps_o[:, :], ALU.mult),
                                  wait=[t_db, P.tok("dG")], inc="eV")
                    P.dma("gpsimd",
                          snd2.ap()[512 * qc + 128 * h:512 * qc + 128 * (h + 1),
                                    512 * b:512 * (b + 1)],
                          attnT[:, :], wait=[t_at])
        t_a2a2 = P.emit("gpsimd", lambda e: _cc_a2a(e, snd2, rcv2),
                        wait=[P.tok("dG"), P.tok("dS")], inc="cc3")
    P.barrier()
    P.flush("s1_C")

    with nc.psum_tensor("s1_pso0", [128, 512], F32) as ps_o0, \
         nc.psum_tensor("s1_pso1", [128, 512], F32) as ps_o1, \
         nc.sbuf_tensor("s1_asb", [128, 32 * 1024], BF16) as attn_sb, \
         nc.sbuf_tensor("s1_wo", [128, 16384], BF16) as wob, \
         nc.sbuf_tensor("s1_ob", [128, 8 * 4096], BF16) as out_bf, \
         nc.sbuf_tensor("s1_y", [128, D], F32) as y_t, \
         nc.sbuf_tensor("s1_x2", [128, D], F32) as x_t2, \
         nc.sbuf_tensor("s1_tmp2", [128, D], BF16) as tmp2:
        for g in range(32):
            P.dma("sync", attn_sb[:, 1024 * g:1024 * (g + 1)],
                  rcv2.ap()[128 * g:128 * (g + 1), :], wait=[("cc3", 1)])
        t_asb = P.tok("dS")
        psos = [ps_o0, ps_o1]
        for dcc in range(8):
            t_w1 = P.dma("gpsimd", wob[:, 0:8192],
                         io["wo_img"][128 * dcc:128 * (dcc + 1), 0:8192],
                         wait=[P.tok("eT")])
            t_w2 = P.dma("gpsimd", wob[:, 8192:16384],
                         io["wo_img"][128 * dcc:128 * (dcc + 1), 8192:16384],
                         wait=[P.tok("eT")])
            t_wo = P.tok("dG")
            for tt in range(8):
                bb, lt = tt // 4, tt % 4
                ps = psos[tt % 2]
                def mm_wo(e, bb=bb, lt=lt, ps=ps):
                    r = None
                    for g in range(32):
                        r = e.matmul(ps[:, :],
                                     attn_sb[:, 1024 * g + 512 * bb + 128 * lt:
                                             1024 * g + 512 * bb + 128 * (lt + 1)],
                                     wob[:, 512 * g:512 * (g + 1)],
                                     start=(g == 0), stop=(g == 31))
                    return r
                t_mm = P.emit("tensor", mm_wo,
                              wait=[t_wo, t_asb, P.tok("eV")], inc="eT")
                P.emit("vector", lambda e, tt=tt, dcc=dcc, ps=ps: e.tensor_copy(
                    out_bf[:, 4096 * tt + 512 * dcc:4096 * tt + 512 * (dcc + 1)], ps[:, :]),
                    wait=[t_mm], inc="eV")
        t_ob = P.tok("eV")
        for tt in range(8):
            t_x = P.dma("sync", x_t2[:, :], x_in[128 * tt:128 * (tt + 1), :],
                        wait=[P.tok("eV")])
            aux = {"colbuf": colbuf,
                   "x_ready": [t_ob, P.tok("eS"), P.tok("eV")]}
            emit_layernorm(P, out_bf[:, 4096 * tt:4096 * (tt + 1)], y_t[:, :], D,
                           tmp2[:, :], aux)
            t_add = P.emit("vector", lambda e: e.tensor_add(y_t[:, :], y_t[:, :], x_t2[:, :]),
                           wait=[aux["t_out"], P.tok("dS")], inc="eV")
            P.dma("sync", x_out[128 * tt:128 * (tt + 1), :], y_t[:, :], wait=[t_add])
        P.emit("sync", lambda e: e.nop(), wait=[P.tok("dS")])
        P.flush("s1_E")


def _cc_a2a(e, snd, rcv):
    from concourse.aot_env import allow_ici_all_to_all
    with allow_ici_all_to_all():
        return e.collective_compute(
            "AllToAll", ALU.bypass, replica_groups=[list(range(8))],
            ins=[snd.ap()], outs=[rcv.ap()])


def build_nc_stage1():
    nc = bass.Bass("TRN2", target_bir_lowering=False)
    io = {}
    io["x_in"] = nc.dram_tensor("x_in", [TPC, D], F32, kind="ExternalInput").ap()
    io["wq_img"] = nc.dram_tensor("wq_img", [48 * 128, 4096], F32, kind="ExternalInput").ap()
    io["wo_img"] = nc.dram_tensor("wo_img", [8 * 128, 16384], F32, kind="ExternalInput").ap()
    io["ropecs"] = nc.dram_tensor("ropecs", [128, 512], F32, kind="ExternalInput").ap()
    io["ropesn"] = nc.dram_tensor("ropesn", [128, 512], F32, kind="ExternalInput").ap()
    io["x_out"] = nc.dram_tensor("x_out", [TPC, D], F32, kind="ExternalOutput").ap()
    io["identf"] = nc.inline_tensor(IDENT_F32, "identf").ap()
    io["identb"] = nc.inline_tensor(IDENT_BF, "identb").ap()
    io["snd"] = nc.dram_tensor("snd", [6144, 1024], BF16, kind="Internal")
    io["rcv"] = nc.dram_tensor("rcv", [6144, 1024], BF16, kind="Internal")
    io["snd2"] = nc.dram_tensor("snd2", [4096, 1024], BF16, kind="Internal")
    io["rcv2"] = nc.dram_tensor("rcv2", [4096, 1024], BF16, kind="Internal")
    with ExitStack() as stack:
        P = Prog(nc, stack)
        build_stage1(P, io)
    return nc


def run_stage1(x_full, w_qkv, w_o):
    nc = build_nc_stage1()
    shards = shard_tokens(np.asarray(x_full, dtype=np.float32))
    wq_img = _wq_images(w_qkv).reshape(48 * 128, 4096)
    wo_img = _wo_images(w_o).reshape(8 * 128, 16384)
    in_maps = []
    for i in range(NCORES):
        cs, sn = _rope_tables(i)
        in_maps.append({"x_in": shards[i], "wq_img": wq_img, "wo_img": wo_img,
                        "ropecs": cs, "ropesn": sn})
    res = run_bass_kernel_spmd(nc, in_maps, core_ids=list(range(NCORES)))
    return unshard_tokens([r["x_out"] for r in res.results])



FCH = 17
NF = FCH * 128
NLAG = 2176


def _s3_consts():
    L = 4096
    t = np.arange(L, dtype=np.float64)
    f = np.arange(NF, dtype=np.float64)
    ang = np.outer(t, f) * (2 * np.pi / L)
    fre = np.cos(ang)
    fim = -np.sin(ang)
    fre[:, 2049:] = 0.0
    fim[:, 2049:] = 0.0
    def img(M):
        out = np.zeros((FCH * 128, L), dtype=np.float64)
        for fc in range(FCH):
            blk = M[:, fc * 128:(fc + 1) * 128]
            b = blk.reshape(32, 128, 128)
            out[fc * 128:(fc + 1) * 128, :] = b.transpose(1, 0, 2).reshape(128, L)
        return out
    fre_i = img(fre).astype(bf16)
    fim_i = img(fim).astype(bf16)
    lag = np.arange(NLAG, dtype=np.float64)
    wt = np.full(NF, 2.0 / L)
    wt[0] = 1.0 / L
    wt[2048] = 1.0 / L
    wt[2049:] = 0.0
    cinv = wt[:, None] * np.cos(np.outer(f, lag) * (2 * np.pi / L))
    cinv[2049:, :] = 0.0
    cinv[:, 2049:] = 0.0
    cinv_i = cinv.astype(bf16)
    return fre_i, fim_i, cinv_i


def build_stage3(P, io):
    nc = P.nc
    st = P.stack
    sb = lambda name, shape, dt=F32: st.enter_context(nc.sbuf_tensor(name, shape, dt))
    x_in, x_out = io["x_in"], io["x_out"]
    dbl = io["dbl"].ap()

    identf = sb("s3_identf", [128, 128], F32)
    t_idf = P.dma("sync", identf[:, :], io["identf"][:, :])
    colbuf = sb("s3_col", [128, 8])
    P_sb = sb("s3_P", [128, FCH * TPC], BF16)
    ps_re = st.enter_context(nc.psum_tensor("s3_psre", [128, 512], F32))
    ps_im = st.enter_context(nc.psum_tensor("s3_psim", [128, 512], F32))
    ps_c = [st.enter_context(nc.psum_tensor(f"s3_psc{i}", [128, 512], F32)) for i in range(5)]

    with nc.sbuf_tensor("s3_lnT", [128, 32 * 512], BF16) as lnT, \
         nc.sbuf_tensor("s3_f1", [128, D], BF16) as fbuf1, \
         nc.sbuf_tensor("s3_f2", [128, D], BF16) as fbuf2, \
         nc.sbuf_tensor("s3_f3", [128, D], BF16) as fbuf3, \
         nc.sbuf_tensor("s3_f4", [128, D], BF16) as fbuf4, \
         nc.sbuf_tensor("s3_x", [128, D], F32) as x_tile, \
         nc.sbuf_tensor("s3_z", [128, D], F32) as z_tile, \
         nc.sbuf_tensor("s3_tmp", [128, D], BF16) as tmp_bf, \
         nc.sbuf_tensor("s3_sq", [128, 512], F32) as sq_t:
        for g in range(2):
            for tt in range(4):
                gt = 4 * g + tt
                t_x = P.dma("sync", x_tile[:, :], x_in[128 * gt:128 * (gt + 1), :],
                            wait=[P.tok("eV"), P.tok("eS")])
                aux = {"colbuf": colbuf, "x_ready": [P.tok("dS"), P.tok("eV"), P.tok("eS")]}
                emit_layernorm(P, x_tile[:, :], z_tile[:, :], D, tmp_bf[:, :], aux)
                P.dma("gpsimd", dbl[128 * gt:128 * (gt + 1), 0:D], z_tile[:, :],
                      wait=[aux["t_out"]])
                P.dma("gpsimd", dbl[128 * gt:128 * (gt + 1), D:2 * D], z_tile[:, :],
                      wait=[aux["t_out"]])
                import os as _os
                if _os.environ.get("S3_SKIP_TR", "0") == "1":
                    continue
                evac_prev = [P.tok("eV"), P.tok("eV")]
                for dch in range(32):
                    ps = ps_re if dch % 2 == 0 else ps_im
                    t_tr = P.emit("tensor", lambda e, dch=dch, ps=ps: e.transpose(
                        ps[:, 0:128], z_tile[:, 128 * dch:128 * (dch + 1)], identf[:, :]),
                        wait=[aux["t_out"], t_idf, evac_prev[dch % 2]], inc="eT")
                    t_ev = P.emit("vector", lambda e, dch=dch, tt=tt, ps=ps: e.tensor_copy(
                        lnT[:, 512 * dch + 128 * tt:512 * dch + 128 * (tt + 1)], ps[:, 0:128]),
                        wait=[t_tr], inc="eV")
                    evac_prev[dch % 2] = t_ev
            t_lnT = P.tok("eV")
            if _os.environ.get("S3_SKIP_DFT", "0") == "1":
                continue
            fb = [fbuf1, fbuf2]
            t_fload = [P.tok("dS"), P.tok("dS")]
            for fc in range(FCH):
                fre_src = io["fre"][128 * fc:128 * (fc + 1), :]
                fim_src = io["fim"][128 * fc:128 * (fc + 1), :]
                b = fc % 2
                t_l1 = P.dma("sync", fbuf1[:, :], fre_src, wait=[P.tok("eT")])
                t_l2 = P.dma("sync", fbuf2[:, :], fim_src, wait=[P.tok("eT")])
                t_w = P.tok("dS")

                def mm_dft(e, ps, fbuf, fc=fc):
                    r = None
                    for tc in range(32):
                        r = e.matmul(ps[:, :], fbuf[:, 128 * tc:128 * (tc + 1)],
                                     lnT[:, 512 * tc:512 * (tc + 1)],
                                     start=(tc == 0), stop=(tc == 31))
                    return r
                if _os.environ.get("S3_DFT_LOADONLY", "0") == "1":
                    continue
                t_mre = P.emit("tensor", lambda e, f=mm_dft: f(e, ps_re, fbuf1),
                               wait=[t_w, t_lnT, P.tok("eS"), P.tok("eV")], inc="eT")
                if _os.environ.get("S3_DFT_ONEMM", "0") == "1":
                    t_cp = P.emit("vector", lambda e, fc=fc, g=g: e.tensor_copy(
                        P_sb[:, fc * TPC + 512 * g:fc * TPC + 512 * (g + 1)], ps_re[:, :]),
                        wait=[t_mre], inc="eV")
                    continue
                t_mim = P.emit("tensor", lambda e, f=mm_dft: f(e, ps_im, fbuf2),
                               wait=[t_w, t_lnT, P.tok("eV")], inc="eT")
                t_s1 = P.emit("scalar", lambda e: e.activation(out=sq_t[:, :], in_=ps_re[:, :],
                                                               func=AF.Square),
                              wait=[t_mre], inc="eS")
                t_s2 = P.emit("scalar", lambda e: e.activation(out=ps_re[:, :], in_=ps_im[:, :],
                                                               func=AF.Square),
                              wait=[t_mim, t_s1], inc="eS")
                t_p = P.emit("vector", lambda e, fc=fc, g=g: e.tensor_tensor(
                    P_sb[:, fc * TPC + 512 * g:fc * TPC + 512 * (g + 1)],
                    sq_t[:, :], ps_re[:, :], ALU.add),
                    wait=[t_s2], inc="eV")
    P.barrier()
    P.flush("s3_A")

    import os as _os
    if _os.environ.get("S3_PHASEA_ONLY", "0") == "1":
        with nc.sbuf_tensor("s3_pass", [128, D], F32) as pt:
            for gt in range(NT):
                t_x = P.dma("sync", pt[:, :], x_in[128 * gt:128 * (gt + 1), :],
                            wait=[P.tok("dS")])
                P.dma("sync", x_out[128 * gt:128 * (gt + 1), :], pt[:, :],
                      wait=[P.tok("dS")])
            P.emit("sync", lambda e: e.nop(), wait=[P.tok("dS")])
            P.flush("s3_stub")
        return

    with nc.sbuf_tensor("s3_cinv", [128, FCH * NLAG], BF16) as cinv_sb, \
         nc.sbuf_tensor("s3_ch", [128, NLAG], F32) as corr_h, \
         nc.sbuf_tensor("s3_cf", [128, 4096], F32) as corr_f, \
         nc.sbuf_tensor("s3_g0", [128, D], F32) as g0, \
         nc.sbuf_tensor("s3_g1", [128, D], F32) as g1, \
         nc.sbuf_tensor("s3_y", [128, D], F32) as y_sb, \
         nc.sbuf_tensor("s3_x2", [128, D], F32) as x_tile2, \
         nc.sbuf_tensor("s3_mx", [128, 8], F32) as mx, \
         nc.sbuf_tensor("s3_ix", [128, 8], U32) as ix, \
         nc.sbuf_tensor("s3_wt", [128, 8], F32) as wt, \
         nc.sbuf_tensor("s3_sc", [128, 8], F32) as sc, \
         nc.sbuf_tensor("s3_idx", [128, 8], mybir.dt.int32) as idxi, \
         nc.sbuf_tensor("s3_wt2", [128, 8], F32) as wt2, \
         nc.sbuf_tensor("s3_iota", [128, 1], F32) as iota_c:
        for fc in range(FCH):
            P.dma("sync", cinv_sb[:, NLAG * fc:NLAG * (fc + 1)],
                  io["cinv"][128 * fc:128 * (fc + 1), :])
        t_cinv = P.tok("dS")
        gathers = [g0, g1]
        for gt in range(NT):
            def mm_corr(e, gt=gt):
                r = None
                g = gt // 4
                q = gt % 4
                for fc in range(FCH):
                    lh = P_sb[:, fc * TPC + 512 * g + 128 * q:fc * TPC + 512 * g + 128 * (q + 1)]
                    for tc in range(5):
                        wdt = 512 if tc < 4 else 128
                        r = e.matmul(ps_c[tc][:, 0:wdt], lh,
                                     cinv_sb[:, NLAG * fc + 512 * tc:NLAG * fc + 512 * tc + wdt],
                                     start=(fc == 0), stop=(fc == FCH - 1))
                return r
            t_cm = P.emit("tensor", mm_corr,
                          wait=[t_cinv, P.tok("eV"), P.tok("eS")], inc="eT")
            t_ce = []
            for tc in range(5):
                wdt = 512 if tc < 4 else 128
                t_ce.append(P.emit("vector", lambda e, tc=tc, wdt=wdt: e.tensor_copy(
                    corr_h[:, 512 * tc:512 * tc + wdt], ps_c[tc][:, 0:wdt]),
                    wait=[t_cm], inc="eV"))
            t_cf1 = P.emit("vector", lambda e: e.tensor_copy(corr_f[:, 0:2048],
                                                             corr_h[:, 0:2048]),
                           wait=t_ce, inc="eV")
            import os as _os
            if _os.environ.get("S3_NOREV", "0") == "1":
                t_cf2 = P.emit("vector", lambda e: e.tensor_copy(
                    corr_f[:, 2048:4096], corr_h[:, 0:2048]),
                    wait=t_ce, inc="eV")
            else:
                t_cf2 = P.emit("vector", lambda e: e.tensor_copy(
                    corr_f[:, 2048:4096], corr_h[:, 2048:0:-1]),
                    wait=t_ce, inc="eV")
            t_m = P.emit("vector", lambda e: e.max(out=mx[:, :], in_=corr_f[:, :]),
                         wait=[t_cf1, t_cf2], inc="eV")
            t_i = P.emit("vector", lambda e: e.max_index(out=ix[:, :], in_max=mx[:, :],
                                                         in_values=corr_f[:, :]),
                         wait=[t_m], inc="eV")
            t_n = P.emit("vector", lambda e: e.tensor_scalar_mul(sc[:, 0:1], mx[:, 0:1], -1.0),
                         wait=[t_m], inc="eV")
            t_e = P.emit("scalar", lambda e: e.activation(out=wt[:, :], in_=mx[:, :], func=AF.Exp,
                                                          bias=sc[:, 0:1],
                                                          accum_out=sc[:, 1:2]),
                         wait=[t_n], inc="eS")
            t_r = P.emit("vector", lambda e: e.reciprocal(sc[:, 2:3], sc[:, 1:2]),
                         wait=[t_e], inc="eV")
            t_wt = P.emit("vector", lambda e: e.tensor_scalar(wt[:, :], wt[:, :], sc[:, 2:3],
                                                              None, op0=ALU.mult),
                          wait=[t_r], inc="eV")
            t_io = P.emit("gpsimd", lambda e, gt=gt: e.iota(
                iota_c[:, :], pattern=[[0, 1]], base=128 * gt * 8192,
                channel_multiplier=8192, allow_small_or_imprecise_dtypes=True),
                wait=[P.tok("dG")], inc="eG")
            t_ii = P.emit("vector", lambda e: e.tensor_copy(sc[:, 3:4].to_broadcast([128, 1]) if False else wt2[:, 0:8], ix[:, :]),
                          wait=[t_i], inc="eV")
            t_ia = P.emit("vector", lambda e: e.tensor_scalar(wt2[:, 0:8], wt2[:, 0:8],
                                                              iota_c[:, 0:1], None, op0=ALU.add),
                          wait=[t_ii, t_io], inc="eV")
            t_ic = P.emit("vector", lambda e: e.tensor_copy(idxi[:, :], wt2[:, 0:8]),
                          wait=[t_ia], inc="eV")
            t_ia = t_ic
            t_prev_tap = None
            for k in range(8):
                gb = gathers[k % 2]
                import os as _os
                if _os.environ.get("S3_NOGATHER", "0") == "1":
                    t_g = P.dma("gpsimd", gb[:, :], dbl[128 * gt:128 * (gt + 1), 0:D],
                                wait=[t_ia, P.tok("eV"), P.tok("eS"), P.tok("dG")])
                else:
                    t_g = P.emit("gpsimd", lambda e, k=k, gb=gb: e.indirect_dma_start(
                        out=gb[:, :], out_offset=None,
                        in_=dbl.rearrange("a b -> (a b)").unsqueeze(1),
                        in_offset=IndirectOffsetOnAxis(idxi[:, k:k + 1], 0)),
                        wait=[t_ia, P.tok("eV"), P.tok("eS"), P.tok("dG")], inc="dG", inc_by=16)
                if k == 0:
                    t_prev_tap = P.emit("vector", lambda e, gb=gb: e.tensor_scalar(
                        y_sb[:, :], gb[:, :], wt[:, 0:1], None, op0=ALU.mult),
                        wait=[t_g, t_wt], inc="eV")
                else:
                    t_prev_tap = P.emit("vector", lambda e, k=k, gb=gb: e.scalar_tensor_tensor(
                        y_sb[:, :], gb[:, :], wt[:, k:k + 1], y_sb[:, :],
                        op0=ALU.mult, op1=ALU.add),
                        wait=[t_g, t_prev_tap], inc="eV")
            t_x = P.dma("sync", x_tile2[:, :], x_in[128 * gt:128 * (gt + 1), :],
                        wait=[P.tok("eV")])
            aux2 = {"colbuf": colbuf,
                    "x_ready": [t_prev_tap, P.tok("eS")]}
            emit_layernorm(P, y_sb[:, :], y_sb[:, :], D, g0[:, :], aux2)
            t_add = P.emit("vector", lambda e: e.tensor_add(y_sb[:, :], y_sb[:, :],
                                                            x_tile2[:, :]),
                           wait=[aux2["t_out"], P.tok("dS")], inc="eV")
            P.dma("sync", x_out[128 * gt:128 * (gt + 1), :], y_sb[:, :], wait=[t_add])
        P.emit("sync", lambda e: e.nop(), wait=[P.tok("dS")])
        P.flush("s3_B")


def build_nc_stage3():
    nc = bass.Bass("TRN2", target_bir_lowering=False)
    fre_i, fim_i, cinv_i = _s3_consts()
    io = {}
    io["x_in"] = nc.dram_tensor("x_in", [TPC, D], F32, kind="ExternalInput").ap()
    io["fre"] = nc.dram_tensor("fre", [FCH * 128, 4096], BF16, kind="ExternalInput").ap()
    io["fim"] = nc.dram_tensor("fim", [FCH * 128, 4096], BF16, kind="ExternalInput").ap()
    io["cinv"] = nc.dram_tensor("cinv", [FCH * 128, NLAG], BF16, kind="ExternalInput").ap()
    io["x_out"] = nc.dram_tensor("x_out", [TPC, D], F32, kind="ExternalOutput").ap()
    io["identf"] = nc.inline_tensor(IDENT_F32, "identf").ap()
    io["dbl"] = nc.dram_tensor("dbl", [TPC, 2 * D], F32, kind="Internal")
    with ExitStack() as stack:
        P = Prog(nc, stack)
        build_stage3(P, io)
        P.flush("s3")
    return nc, (fre_i, fim_i, cinv_i)


def run_stage3(x2_full):
    nc, (fre_i, fim_i, cinv_i) = build_nc_stage3()
    shards = shard_tokens(np.asarray(x2_full, dtype=np.float32))
    in_maps = [{"x_in": s, "fre": fre_i, "fim": fim_i, "cinv": cinv_i} for s in shards]
    res = run_bass_kernel_spmd(nc, in_maps, core_ids=list(range(NCORES)))
    return unshard_tokens([r["x_out"] for r in res.results])



_TRIPLES = [(1, 2, 3), (1, 4, 5), (1, 7, 6), (2, 4, 6), (2, 5, 7), (3, 4, 7), (3, 6, 5)]


def _oct_struct():
    S = np.zeros((8, 8, 8), np.float32)
    S[0, 0, 0] = 1.0
    for i in range(1, 8):
        S[0, i, i] = 1.0
        S[i, 0, i] = 1.0
        S[i, i, 0] = -1.0
    for a, b, c in _TRIPLES:
        for x, y, z in ((a, b, c), (b, c, a), (c, a, b)):
            S[x, y, z] = 1.0
            S[y, x, z] = -1.0
    return S


def _oct_tables():
    S = _oct_struct()
    j_tab = np.zeros((8, 8), np.int64)
    s_tab = np.zeros((8, 8), np.float32)
    for i in range(8):
        for k in range(8):
            js = np.nonzero(S[:, i, k])[0]
            assert len(js) == 1, (i, k, js)
            j_tab[i, k] = js[0]
            s_tab[i, k] = S[js[0], i, k]
    return j_tab, s_tab


J_TAB, S_TAB = _oct_tables()
HID = 10928
OC_N = 11
OCW = [128] * 10 + [86]
OC_BASE = [128 * i for i in range(11)]
GIMG = 699392
WQ_FLAT = 3 * GIMG
N_W_ELEM = 8 * 1366 * 512


def _g_img_off(oc):
    return sum(128 * 4 * OCW[o] for o in range(oc))


def build_stage4(P, io):
    nc = P.nc
    st = P.stack
    sb = lambda name, shape, dt=F32: st.enter_context(nc.sbuf_tensor(name, shape, dt))

    x_in, x_out = io["x_in"], io["x_out"]
    wq_mine = io["wq_mine"]
    wq_all = io["wq_all"]

    ident = sb("s4_ident", [128, 128], BF16)
    identf = sb("s4_identf", [128, 128], F32)
    t_ident = P.dma("sync", ident[:, :], io["ident"][:, :])
    t_identf = P.dma("sync", identf[:, :], io["identf"][:, :])

    with nc.psum_tensor("s4q_pt", [128, 128], BF16) as ps_t, \
         nc.psum_tensor("s4q_pt2", [128, 128], BF16) as ps_t2, \
         nc.psum_tensor("s4q_red", [4, 8], F32) as ps_red, \
         nc.sbuf_tensor("s4q_w", [128, 1366], F32) as wtile, \
         nc.sbuf_tensor("s4q_q", [128, 1366], BF16) as qtile, \
         nc.sbuf_tensor("s4q_q2", [128, 1366], BF16) as qtile2, \
         nc.sbuf_tensor("s4q_t", [128, 128], BF16) as ttile, \
         nc.sbuf_tensor("s4q_acc", [128, 4], F32) as acc, \
         nc.sbuf_tensor("s4q_redsb", [4, 2], F32) as red, \
         nc.sbuf_tensor("s4q_scl", [128, 8], F32) as sclq:
        scl = sclq
        ones = P.konst(1.0)

        shards = [("wg", io["wg"], 11, 512, 0), ("wu", io["wu"], 11, 512, GIMG),
                  ("wd", io["wd"], 4, 1366, 2 * GIMG)]
        t_acc = []
        for mi, (nm, w_ap, ntile, cols, _off) in enumerate(shards):
            tz = P.emit("vector", lambda e, mi=mi: e.memset(acc[:, mi:mi + 1], 0.0), inc="eV")
            tprev = tz
            for ti in range(ntile):
                rows = min(128, w_ap.shape[0] - 128 * ti)
                t_l = P.dma("sync", wtile[0:rows, 0:cols], w_ap[128 * ti:128 * ti + rows, :],
                            wait=[tprev])
                t_r = P.emit("vector", lambda e, rows=rows, cols=cols: e.tensor_reduce(
                    acc[0:rows, 3:4], wtile[0:rows, 0:cols], axis=AX.X, op=ALU.add,
                    apply_absolute_value=True), wait=[P.tok("dS")], inc="eV")
                tprev = P.emit("vector", lambda e, rows=rows, mi=mi: e.tensor_tensor(
                    acc[0:rows, mi:mi + 1], acc[0:rows, mi:mi + 1], acc[0:rows, 3:4], ALU.add),
                    wait=[t_r], inc="eV")
            t_acc.append(tprev)
        t_mm = P.emit("tensor", lambda e: e.matmul(ps_red[0:3, 0:1], acc[:, 0:3], ones,
                                                   start=True, stop=True),
                      wait=t_acc + [P.ktok], inc="eT")
        t_red0 = P.emit("vector", lambda e: e.memset(red[0:4, 0:2], 0.0), inc="eV")
        t_red = P.emit("vector", lambda e: e.tensor_copy(red[0:3, 0:1], ps_red[0:3, 0:1]),
                       wait=[t_mm, t_red0], inc="eV")
        P.dma("sync", io["ar_in"].ap()[0:4, 0:1], red[0:4, 0:1], wait=[t_red])
        t_ar = P.emit("gpsimd", lambda e: e.collective_compute(
            "AllReduce", ALU.add, replica_groups=[list(range(8))],
            ins=[io["ar_in"].ap()], outs=[io["ar_out"].ap()]),
            wait=[P.tok("dS")], inc="cc0")
        t_sl = P.dma("sync", scl[:, 0:4],
                     io["ar_out"].ap().rearrange("a b -> (a b)").unsqueeze(0).to_broadcast([128, 4]),
                     wait=[t_ar])
        t_s1 = P.emit("vector", lambda e: e.tensor_scalar_mul(scl[:, 0:4], scl[:, 0:4],
                                                              1.0 / N_W_ELEM),
                      wait=[P.tok("dS")], inc="eV")
        t_s2 = P.emit("vector", lambda e: e.reciprocal(scl[:, 4:8], scl[:, 0:4]),
                      wait=[t_s1], inc="eV")

        for mi, (nm, w_ap, ntile, cols, off) in enumerate(shards):
            for ti in range(ntile):
                rows = min(128, w_ap.shape[0] - 128 * ti)
                t_l = P.dma("sync", wtile[0:rows, 0:cols], w_ap[128 * ti:128 * ti + rows, :],
                            wait=[P.tok("eS"), P.tok("eV"), P.tok("eT")])
                wld = [P.tok("dS"), t_s2]
                t_q1 = P.emit("vector", lambda e, rows=rows, cols=cols, mi=mi: e.tensor_scalar(
                    wtile[0:rows, 0:cols], wtile[0:rows, 0:cols], scl[0:rows, 4 + mi:5 + mi], None,
                    op0=ALU.mult), wait=wld, inc="eV")
                t_q2 = P.emit("vector", lambda e, rows=rows, cols=cols: e.tensor_scalar(
                    qtile[0:rows, 0:cols], wtile[0:rows, 0:cols], 0.5, None, op0=ALU.is_gt),
                    wait=[t_q1], inc="eV")
                t_q3 = P.emit("vector", lambda e, rows=rows, cols=cols: e.tensor_scalar(
                    qtile2[0:rows, 0:cols], wtile[0:rows, 0:cols], -0.5, None, op0=ALU.is_lt),
                    wait=[t_q1], inc="eV")
                t_q5 = P.emit("vector", lambda e, rows=rows, cols=cols: e.tensor_sub(
                    qtile[0:rows, 0:cols], qtile[0:rows, 0:cols], qtile2[0:rows, 0:cols]),
                    wait=[t_q2, t_q3], inc="eV")
                t_q6 = P.emit("vector", lambda e, rows=rows, cols=cols, mi=mi: e.tensor_scalar(
                    qtile[0:rows, 0:cols], qtile[0:rows, 0:cols], scl[0:rows, mi:mi + 1], None,
                    op0=ALU.mult), wait=[t_q5], inc="eV")
                if nm != "wd":
                    for dch in range(4):
                        ps = ps_t if dch % 2 == 0 else ps_t2
                        t_tr = P.emit("tensor", lambda e, dch=dch, rows=rows, ps=ps: e.transpose(
                            ps[0:128, 0:rows], qtile[0:rows, 128 * dch:128 * (dch + 1)],
                            ident[0:rows, 0:rows]),
                            wait=[t_q6, t_ident, P.tok("dG")], inc="eT")
                        t_ev = P.emit("vector", lambda e, rows=rows, ps=ps: e.tensor_copy(
                            ttile[0:128, 0:rows], ps[0:128, 0:rows]),
                            wait=[t_tr, P.tok("dG")], inc="eV")
                        ocw = rows
                        base = off + _g_img_off(ti)
                        dst = wq_mine.ap().rearrange("a b -> (a b)")
                        dst2 = dst[base:base + 128 * 4 * ocw].rearrange(
                            "(p c) -> p c", c=4 * ocw)[0:128, dch * ocw:(dch + 1) * ocw]
                        P.dma("gpsimd", dst2, ttile[0:128, 0:rows], wait=[t_ev])
                else:
                    for oc in range(OC_N):
                        ocw = OCW[oc]
                        ps = ps_t if oc % 2 == 0 else ps_t2
                        t_tr = P.emit("tensor", lambda e, oc=oc, ocw=ocw, rows=rows, ps=ps:
                                      e.transpose(ps[0:ocw, 0:rows],
                                                  qtile[0:rows, OC_BASE[oc]:OC_BASE[oc] + ocw],
                                                  ident[:, :]),
                                      wait=[t_q6, t_ident, P.tok("dG")], inc="eT")
                        t_ev = P.emit("vector", lambda e, ocw=ocw, rows=rows, ps=ps: e.tensor_copy(
                            ttile[0:ocw, 0:rows], ps[0:ocw, 0:rows]),
                            wait=[t_tr, P.tok("dG")], inc="eV")
                        dst = wq_mine.ap().rearrange("a b -> (a b)")
                        region = dst[2 * GIMG + OC_BASE[oc] * 512:
                                     2 * GIMG + (OC_BASE[oc] + ocw) * 512].rearrange(
                            "(r c) -> r c", c=512)
                        dst3 = region[0:ocw, 128 * ti:128 * ti + rows]
                        P.dma("gpsimd", dst3, ttile[0:ocw, 0:rows], wait=[t_ev])
        t_ag = P.emit("gpsimd", lambda e: e.collective_compute(
            "AllGather", ALU.bypass, replica_groups=[list(range(8))],
            ins=[wq_mine.ap()], outs=[wq_all.ap()]),
            wait=[P.tok("dG"), P.tok("dS")], inc="cc1")
        P.barrier()
        P.flush("s4_quant")

    with ExitStack() as mst:
        msb = lambda name, shape, dt=F32: mst.enter_context(nc.sbuf_tensor(name, shape, dt))
        ps_g = mst.enter_context(nc.psum_tensor("s4_psg", [128, 512], F32))
        ps_u = mst.enter_context(nc.psum_tensor("s4_psu", [128, 512], F32))
        ps_d = [mst.enter_context(nc.psum_tensor(f"s4_psd{i}", [128, 512], F32))
                for i in range(6)]
        ps_down = [ps_g, ps_u] + ps_d

        ln4T = msb("s4_ln4T", [128, 32 * 512], BF16)
        hT = msb("s4_hT", [128, 88 * 512], BF16)
        hTn = msb("s4_hTn", [128, 88 * 128], BF16)
        gslc = msb("s4_gslc", [128, 2 * 4096], BF16)
        dslc = msb("s4_dslc", [128, 8 * 512], BF16)
        gsil = msb("s4_gsil", [128, 8 * 512], BF16)
        out_t = msb("s4_out", [128, D])
        z_tile = out_t
        tmp_bf = gslc[:, 0:D]
        colbuf = msb("s4_col", [128, 8])
        x_tile = msb("s4_xt", [128, D])

        for g in range(2):
            for tt in range(4):
                gt = 4 * g + tt
                t_x = P.dma("sync", out_t[:, :], x_in[128 * gt:128 * (gt + 1), :],
                            wait=[P.tok("eV"), P.tok("eS"), P.tok("eT"), P.tok("dG")])
                aux = {"colbuf": colbuf,
                       "x_ready": [P.tok("dS"), P.tok("eT"), P.tok("eV")]}
                emit_layernorm(P, out_t[:, :], z_tile[:, :], D, tmp_bf[:, :], aux)
                evac_prev = [P.tok("eV"), P.tok("eV")]
                for dch in range(32):
                    ps = ps_g if dch % 2 == 0 else ps_u
                    t_tr = P.emit("tensor", lambda e, dch=dch, ps=ps: e.transpose(
                        ps[:, 0:128], z_tile[:, 128 * dch:128 * (dch + 1)], identf[:, :]),
                        wait=[aux["t_out"], evac_prev[dch % 2]], inc="eT")
                    t_ev = P.emit("vector", lambda e, dch=dch, tt=tt, ps=ps: e.tensor_copy(
                        ln4T[:, 512 * dch + 128 * tt:512 * dch + 128 * (tt + 1)], ps[:, 0:128]),
                        wait=[t_tr], inc="eV")
                    evac_prev[dch % 2] = t_ev
            t_ln4T = P.tok("eV")

            for oc in range(OC_N):
                ocw = OCW[oc]
                w = 4 * ocw

                def load_slices(mat_off):
                    toks_w = [P.tok("eT"), P.tok("eS"), ("cc1", 1)]
                    for j in range(8):
                        src_off = mat_off + _g_img_off(oc)
                        src = wq_all.ap()[j:j + 1, src_off:src_off + 128 * w].rearrange(
                            "a (p c) -> (a p) c", p=128)
                        P.dma("sync", gslc[:, j * w:(j + 1) * w], src, wait=toks_w)
                    t_w = P.tok("dS")
                    return P.emit("vector", lambda e, w=w: e.tensor_scalar_mul(
                        gslc[:, 4096:4096 + 8 * w], gslc[:, 0:8 * w], -1.0),
                        wait=[t_w, P.tok("eT")], inc="eV")

                def mm_gu(e, k, ps, ocw=ocw, w=w):
                    r = None
                    for i in range(8):
                        j = int(J_TAB[i, k])
                        neg = 4096 if S_TAB[i, k] < 0 else 0
                        for dch in range(4):
                            r = e.matmul(
                                ps[0:ocw, :],
                                gslc[:, neg + j * w + dch * ocw: neg + j * w + (dch + 1) * ocw],
                                ln4T[:, 512 * (4 * i + dch):512 * (4 * i + dch + 1)],
                                start=(i == 0 and dch == 0), stop=(i == 7 and dch == 3))
                    return r

                t_ng = load_slices(0)
                t_sil = []
                for k in range(8):
                    ps = ps_g if k % 2 == 0 else ps_u
                    t_gmm = P.emit("tensor", lambda e, k=k, ps=ps, f=mm_gu: f(e, k, ps),
                                   wait=[t_ng, t_ln4T, P.tok("eS"), P.tok("eV")], inc="eT")
                    t_sil.append(P.emit("scalar", lambda e, k=k, ocw=ocw, ps=ps: e.activation(
                        out=gsil[0:ocw, 512 * k:512 * (k + 1)], in_=ps[0:ocw, :], func=AF.Silu),
                        wait=[t_gmm], inc="eS"))
                t_nu = load_slices(GIMG)
                for k in range(8):
                    hcol = (k * OC_N + oc) * 512
                    ps = ps_g if k % 2 == 0 else ps_u
                    t_umm = P.emit("tensor", lambda e, k=k, ps=ps, f=mm_gu: f(e, k, ps),
                                   wait=[t_nu, t_ln4T, P.tok("eV"), P.tok("eS")], inc="eT")
                    t_h = P.emit("vector", lambda e, k=k, ocw=ocw, hcol=hcol, ps=ps:
                                 e.tensor_tensor(hT[0:ocw, hcol:hcol + 512],
                                                 gsil[0:ocw, 512 * k:512 * (k + 1)],
                                                 ps[0:ocw, :], ALU.mult),
                                 wait=[t_umm, t_sil[k]], inc="eV")
            t_hT = P.tok("eV")
            if "dbg_hT" in io and g == 0:
                P.dma("sync", io["dbg_hT"][:, :], hT[:, 0:512], wait=[t_hT])
                P.dma("sync", io["dbg_ln4T"][:, :], ln4T[:, :], wait=[t_hT])
                P.dma("sync", io["dbg_gsil"][:, :], gsil[:, 0:512], wait=[t_hT])

            for tt in range(4):
                gt = 4 * g + tt
                hsrc = hT[:, :].rearrange("p (c t) -> p c t", c=88)[:, :, 128 * tt:128 * (tt + 1)]
                t_hn = P.emit("vector", lambda e, hsrc=hsrc: e.tensor_scalar_mul(
                    hTn[:, :].rearrange("p (c t) -> p c t", c=88), hsrc, -1.0),
                    wait=[t_hT, P.tok("eT")], inc="eV")
                t_ev_prev = [P.tok("eV")]
                mm_done = None
                for oc in range(OC_N):
                    ocw = OCW[oc]
                    for j in range(8):
                        src_off = 2 * GIMG + OC_BASE[oc] * 512
                        src = wq_all.ap()[j:j + 1, src_off:src_off + ocw * 512].rearrange(
                            "a (r c) -> (a r) c", c=512)
                        P.dma("sync", dslc[0:ocw, 512 * j:512 * (j + 1)], src,
                              wait=[P.tok("eT"), ("cc1", 1)])
                    t_wd = P.tok("dS")
                    def mm_down(e, oc=oc, ocw=ocw, tt=tt):
                        r = None
                        for k2 in range(8):
                            for k in range(8):
                                j = int(J_TAB[k, k2])
                                neg = S_TAB[k, k2] < 0
                                lh = (hTn if neg else hT)
                                if neg:
                                    lcol = (k * OC_N + oc) * 128
                                    lhs = hTn[0:ocw, lcol:lcol + 128]
                                else:
                                    lcol = (k * OC_N + oc) * 512 + 128 * tt
                                    lhs = hT[0:ocw, lcol:lcol + 128]
                                r = e.matmul(ps_down[k2][:, :], lhs,
                                             dslc[0:ocw, 512 * j:512 * (j + 1)],
                                             start=(oc == 0 and k == 0),
                                             stop=(oc == OC_N - 1 and k == 7))
                        return r
                    mm_done = P.emit("tensor", mm_down,
                                     wait=[t_hn, t_wd] + t_ev_prev, inc="eT")
                    t_ev_prev = []
                t_ev = []
                for k2 in range(8):
                    t_ev.append(P.emit("vector", lambda e, k2=k2: e.tensor_copy(
                        out_t[:, 512 * k2:512 * (k2 + 1)], ps_down[k2][:, :]),
                        wait=[mm_done], inc="eV"))
                t_x2 = P.dma("sync", x_tile[:, :], x_in[128 * gt:128 * (gt + 1), :],
                             wait=[P.tok("eV")])
                aux2 = {"colbuf": colbuf,
                        "x_ready": t_ev + [P.tok("eT"), P.tok("dG"), P.tok("dS")]}
                emit_layernorm(P, out_t[:, :], out_t[:, :], D, tmp_bf[:, :], aux2)
                t_add = P.emit("vector", lambda e: e.tensor_add(out_t[:, :], out_t[:, :],
                                                                x_tile[:, :]),
                               wait=[aux2["t_out"], P.tok("dS")], inc="eV")
                P.dma("sync", x_out[128 * gt:128 * (gt + 1), :], out_t[:, :], wait=[t_add])
        P.emit("sync", lambda e: e.nop(), wait=[P.tok("dS")])
        P.flush("s4_main")


def build_nc_stage4():
    nc = bass.Bass("TRN2", target_bir_lowering=False)
    io = {}
    io["x_in"] = nc.dram_tensor("x_in", [TPC, D], F32, kind="ExternalInput").ap()
    io["wg"] = nc.dram_tensor("wg", [1366, 512], F32, kind="ExternalInput").ap()
    io["wu"] = nc.dram_tensor("wu", [1366, 512], F32, kind="ExternalInput").ap()
    io["wd"] = nc.dram_tensor("wd", [512, 1366], F32, kind="ExternalInput").ap()
    io["x_out"] = nc.dram_tensor("x_out", [TPC, D], F32, kind="ExternalOutput").ap()
    io["ident"] = nc.inline_tensor(IDENT_BF, "identc").ap()
    io["identf"] = nc.inline_tensor(IDENT_F32, "identf").ap()
    io["wq_mine"] = nc.dram_tensor("wq_mine", [1, WQ_FLAT], BF16, kind="Internal")
    io["wq_all"] = nc.dram_tensor("wq_all", [8, WQ_FLAT], BF16, kind="Internal", addr_space="Shared")
    io["ar_in"] = nc.dram_tensor("ar_in", [4, 1], F32, kind="Internal")
    io["ar_out"] = nc.dram_tensor("ar_out", [4, 1], F32, kind="Internal")
    with ExitStack() as stack:
        P = Prog(nc, stack)
        build_stage4(P, io)
        P.flush("s4")
    return nc


def run_stage4(x3_full, w_gate, w_up, w_down):
    nc = build_nc_stage4()
    shards = shard_tokens(np.asarray(x3_full, dtype=np.float32))
    wg = np.ascontiguousarray(np.asarray(w_gate, dtype=np.float32))
    wu = np.ascontiguousarray(np.asarray(w_up, dtype=np.float32))
    wd = np.ascontiguousarray(np.asarray(w_down, dtype=np.float32))
    in_maps = [{
        "x_in": shards[i], "wg": wg[i], "wu": wu[i], "wd": wd[i],
    } for i in range(NCORES)]
    res = run_bass_kernel_spmd(nc, in_maps, core_ids=list(range(NCORES)))
    return unshard_tokens([r["x_out"] for r in res.results])


def _stage_scoped(P, builder, io):
    outer = P.stack
    with ExitStack() as st:
        P.stack = st
        builder(P, io)
    P.stack = outer
    P.barrier()


def build_nc_full():
    nc = bass.Bass("TRN2", target_bir_lowering=False)
    fre_i, fim_i, cinv_i = _s3_consts()
    identf = nc.inline_tensor(IDENT_F32, "identf").ap()
    identb = nc.inline_tensor(IDENT_BF, "identb").ap()
    ein = lambda n, shp, dt=F32: nc.dram_tensor(n, shp, dt, kind="ExternalInput").ap()
    x_in = ein("x_in", [TPC, D])
    x1 = nc.dram_tensor("x1buf", [TPC, D], F32, kind="Internal").ap()
    x2 = nc.dram_tensor("x2buf", [TPC, D], F32, kind="Internal").ap()
    x3 = nc.dram_tensor("x3buf", [TPC, D], F32, kind="Internal").ap()
    x_out = nc.dram_tensor("x_out", [TPC, D], F32, kind="ExternalOutput").ap()

    io1 = {"x_in": x_in, "x_out": x1, "identf": identf, "identb": identb,
           "wq_img": ein("wq_img", [48 * 128, 4096]),
           "wo_img": ein("wo_img", [8 * 128, 16384]),
           "ropecs": ein("ropecs", [128, 512]), "ropesn": ein("ropesn", [128, 512]),
           "snd": nc.dram_tensor("snd", [6144, 1024], BF16, kind="Internal"),
           "rcv": nc.dram_tensor("rcv", [6144, 1024], BF16, kind="Internal"),
           "snd2": nc.dram_tensor("snd2", [4096, 1024], BF16, kind="Internal"),
           "rcv2": nc.dram_tensor("rcv2", [4096, 1024], BF16, kind="Internal")}
    io2 = {"x_in": x1, "x_out": x2, "ident": identb,
           "w_cat": ein("w_cat", [D, 24]), "bases": ein("bases", [1, 24]),
           "gate": ein("gate", [1, 1]), "w_branch": ein("w_branch", [1024, 1024]),
           "eye_row": ein("eye_row", [1, 16])}
    io3 = {"x_in": x2, "x_out": x3, "identf": identf,
           "fre": ein("fre", [FCH * 128, 4096], BF16), "fim": ein("fim", [FCH * 128, 4096], BF16),
           "cinv": ein("cinv", [FCH * 128, NLAG], BF16),
           "dbl": nc.dram_tensor("dbl", [TPC, 2 * D], F32, kind="Internal")}
    io4 = {"x_in": x3, "x_out": x_out, "ident": identb, "identf": identf,
           "wg": ein("wg", [1366, 512]), "wu": ein("wu", [1366, 512]),
           "wd": ein("wd", [512, 1366]),
           "wq_mine": nc.dram_tensor("wq_mine", [1, WQ_FLAT], BF16, kind="Internal"),
           "wq_all": nc.dram_tensor("wq_all", [8, WQ_FLAT], BF16, kind="Internal",
                                    addr_space="Shared"),
           "ar_in": nc.dram_tensor("ar_in", [4, 1], F32, kind="Internal"),
           "ar_out": nc.dram_tensor("ar_out", [4, 1], F32, kind="Internal")}
    with ExitStack() as stack:
        P = Prog(nc, stack)
        _stage_scoped(P, build_stage1, io1)
        _stage_scoped(P, build_stage2, io2)
        _stage_scoped(P, build_stage3, io3)
        _stage_scoped(P, build_stage4, io4)
        P.flush("fin")
    return nc, (fre_i, fim_i, cinv_i)


def kernel(**inputs):
    nc, (fre_i, fim_i, cinv_i) = build_nc_full()
    shards = shard_tokens(np.asarray(inputs["L0_attn_x"], dtype=np.float32))
    wq_img = _wq_images(inputs["L0_attn_w_qkv"]).reshape(48 * 128, 4096)
    wo_img = _wo_images(inputs["L0_attn_w_o"]).reshape(8 * 128, 16384)
    wcat = _np_wcat(np.asarray(inputs["L0_ffn_W_res"]), np.asarray(inputs["L0_ffn_W_pre"]),
                    np.asarray(inputs["L0_ffn_W_post"]))
    bases = _np_bases(inputs["L0_ffn_H_res_base"], inputs["L0_ffn_H_pre_base"],
                      inputs["L0_ffn_H_post_base"])
    gate = np.asarray(inputs["L0_ffn_gate"], dtype=np.float32).reshape(1, 1)
    wbr = np.ascontiguousarray(np.asarray(inputs["L0_ffn_W_branch"], dtype=np.float32))
    wg = np.ascontiguousarray(np.asarray(inputs["L1_ffn_w_gate"], dtype=np.float32))
    wu = np.ascontiguousarray(np.asarray(inputs["L1_ffn_w_up"], dtype=np.float32))
    wd = np.ascontiguousarray(np.asarray(inputs["L1_ffn_w_down"], dtype=np.float32))
    in_maps = []
    for i in range(NCORES):
        cs, sn = _rope_tables(i)
        in_maps.append({
            "x_in": shards[i], "wq_img": wq_img, "wo_img": wo_img,
            "ropecs": cs, "ropesn": sn,
            "w_cat": wcat, "bases": bases, "gate": gate, "w_branch": wbr,
            "eye_row": EYE4_ROW,
            "fre": fre_i, "fim": fim_i, "cinv": cinv_i,
            "wg": wg[i], "wu": wu[i], "wd": wd[i],
        })
    res = run_bass_kernel_spmd(nc, in_maps, core_ids=list(range(NCORES)))
    out = unshard_tokens([r["x_out"] for r in res.results])
    return np.ascontiguousarray(out.astype(np.float32))


def kernel_unfused(**inputs):
    x1 = run_stage1(inputs["L0_attn_x"], inputs["L0_attn_w_qkv"], inputs["L0_attn_w_o"])
    x2 = run_stage2(x1, inputs["L0_ffn_W_res"], inputs["L0_ffn_W_pre"], inputs["L0_ffn_W_post"],
                    inputs["L0_ffn_H_res_base"], inputs["L0_ffn_H_pre_base"],
                    inputs["L0_ffn_H_post_base"], inputs["L0_ffn_gate"], inputs["L0_ffn_W_branch"])
    x3 = run_stage3(x2)
    out = run_stage4(x3, inputs["L1_ffn_w_gate"], inputs["L1_ffn_w_up"], inputs["L1_ffn_w_down"])
    return np.ascontiguousarray(out.astype(np.float32))
